# version 10
# baseline (speedup 1.0000x reference)
import numpy as np
import concourse.bass as bass
import concourse.mybir as mybir
from concourse.bass_utils import run_bass_kernel_spmd

F32 = mybir.dt.float32
BF16 = mybir.dt.bfloat16
FP8 = mybir.dt.float8e4
OP = mybir.AluOpType
AF = mybir.ActivationFunctionType
AX = mybir.AxisListType.X

MAGIC = 3.0 * 2.0 ** 22
B_GLOB, T, D, H, DH, PD = 16, 2048, 1024, 16, 64, 6
NCORES = 8
B_LOC = B_GLOB // NCORES
NT = T // 128
INNER = 4096
QKV_SCALE = float(DH) ** -0.5
LN_EPS = 1e-5

ENGS = ("pe", "ve", "sc", "gp", "sy")


class Prog:
    def __init__(self, nc):
        self.nc = nc
        self.ops = {e: [] for e in ENGS}
        self.cnt = {}
        self._guards = []
        self.sems = {}
        self.dbg_outs = []
        self.dma_idx = 0
        self.dmg_idx = 0

    def sem(self, name):
        if name not in self.sems:
            g = self.nc.semaphore(name)
            self.sems[name] = g.__enter__()
            self._guards.append(g)
            self.cnt[name] = 0
        return name

    def val(self, name):
        self.sem(name)
        return self.cnt[name]

    CHAINED = {"ve": "ve", "sc": "sc", "gp": "gp"}

    def _emit(self, eng, fn, wait, inc, incn):
        for s, _ in wait:
            self.sem(s)
        wait = list(wait)
        chain = self.CHAINED.get(eng)
        if chain is not None and fn is not None:
            self.sem(chain)
            wait.append((chain, self.cnt[chain]))
            if inc is None:
                inc, incn = chain, 1
        if inc is not None:
            self.sem(inc)
            self.cnt[inc] += incn
        self.ops[eng].append([fn, tuple(wait), inc, incn])

    def mark(self, eng, inc="pe", incn=1):
        spec = self.ops[eng][-1]
        if spec[2] == inc:
            return
        assert spec[2] is None, (eng, spec[2])
        self.sem(inc)
        self.cnt[inc] += incn
        spec[2] = inc
        spec[3] = incn

    @staticmethod
    def _mkop(spec, sems):
        fn, w, inc, incn = spec

        def op(e):
            for s, v in w:
                if v > 0:
                    e.wait_ge(sems[s], v)
            ins = fn(e) if fn is not None else None
            if inc is not None:
                if ins is None:
                    ins = e.nop()
                ins.then_inc(sems[inc], incn)
        return op

    DMA_RING = 6

    def dma(self, eng, fn, wait=()):
        if eng == "gp":
            s = self.sem(f"dmg{self.dmg_idx % 2}")
            self.dmg_idx += 1
        else:
            s = self.sem(f"dma{self.dma_idx % self.DMA_RING}")
            self.dma_idx += 1
        w = list(wait) + [(s, self.cnt[s])]
        self._emit(eng, fn, w, s, 16)
        return (s, self.cnt[s])

    def dma_big(self, k, fn, wait=()):
        s = self.sem(f"dmw{k % 8}")
        w = list(wait) + [(s, self.cnt[s])]
        self._emit("sy", fn, w, s, 16)
        return (s, self.cnt[s])

    def dma_all(self):
        out = [(f"dma{i}", self.val(f"dma{i}")) for i in range(self.DMA_RING)]
        out += [(f"dmg{i}", self.val(f"dmg{i}")) for i in range(2)]
        out += [(f"dmw{i}", self.val(f"dmw{i}")) for i in range(8)]
        return [t for t in out if t[1] > 0]

    def pe(self, fn, wait=(), inc=None, incn=1):
        self._emit("pe", fn, wait, inc, incn)

    def ve(self, fn, wait=(), inc=None, incn=1):
        self._emit("ve", fn, wait, inc, incn)

    def sc(self, fn, wait=(), inc=None, incn=1):
        self._emit("sc", fn, wait, inc, incn)

    def gp(self, fn, wait=(), inc=None, incn=1):
        self._emit("gp", fn, wait, inc, incn)

    def sy(self, fn, wait=(), inc=None, incn=1):
        self._emit("sy", fn, wait, inc, incn)

    def barrier(self):
        own = {"ve": ("ve", self.val("ve")), "sc": ("sc", self.val("sc")),
               "gp": ("gp", self.val("gp")), "pe": ("pe", self.val("pe")),
               "sy": None}
        vals = {}
        for e in ENGS:
            s = self.sem("bar_" + e)
            self.cnt[s] += 1
            vals[e] = self.cnt[s]
        for e in ENGS:
            pre = own[e]
            waits = [("bar_" + o, vals[o]) for o in ENGS if o != e]

            def op(eng, waits=waits, s="bar_" + e, pre=pre):
                if pre is not None and pre[1] > 0:
                    eng.wait_ge(self.sems[pre[0]], pre[1])
                eng.nop().then_inc(self.sems[s], 1)
                for so, v in waits:
                    eng.wait_ge(self.sems[so], v)
            self.ops[e].append([None, (), None, 1, op])

    def build_block(self):
        nc = self.nc

        def runner(eng_ops, e):
            for spec in eng_ops:
                if len(spec) == 5:
                    spec[4](e)
                else:
                    self._mkop(spec, self.sems)(e)

        with nc.Block() as block:
            @block.tensor
            def _(t):
                runner(self.ops["pe"], t)

            @block.vector
            def _(v):
                runner(self.ops["ve"], v)

            @block.scalar
            def _(s):
                runner(self.ops["sc"], s)

            @block.gpsimd
            def _(g):
                runner(self.ops["gp"], g)

            @block.sync
            def _(s):
                runner(self.ops["sy"], s)


def qpieces(start, end, edge=512):
    out, cur = [], start
    while cur < end:
        nxt = min(end, (cur // edge + 1) * edge)
        out.append((cur, nxt))
        cur = nxt
    return out


def build_core(nc, stop_after=None, taps=()):
    P = Prog(nc)
    tens = {}

    ARENA_SZ = 211008
    BASE = (nc.sbuf_base + 63) // 64 * 64
    nc.alloc_sbuf_tensor("arena_reserve",
                         [128, BASE - nc.sbuf_base + ARENA_SZ], mybir.dt.uint8)

    def at(name, off, shape, dtype):
        ap = nc.alloc_sbuf_tensor_at(name, shape, dtype, offset=BASE + off).ap()
        tens[name] = ap
        return ap

    x_d = nc.dram_tensor("x", [B_LOC, T, D], F32, kind="ExternalInput").ap()
    y_d = nc.dram_tensor("y", [B_LOC, T, D], F32, kind="ExternalOutput").ap()
    w_qkv = nc.dram_tensor("w_qkv", [3 * D, D], F32, kind="ExternalInput").ap()
    b_qkv = nc.dram_tensor("b_qkv", [3 * D], F32, kind="ExternalInput").ap()
    w_wl = {n: nc.dram_tensor("w_" + n, [64, D], F32, kind="ExternalInput").ap()
            for n in ("w1", "w2", "r1", "r2")}
    w_mv = nc.dram_tensor("w_mv", [D, D], F32, kind="ExternalInput").ap()
    b_mv = nc.dram_tensor("b_mv", [D], F32, kind="ExternalInput").ap()
    w_mg = nc.dram_tensor("w_mg", [H, D], F32, kind="ExternalInput").ap()
    b_mg = nc.dram_tensor("b_mg", [H], F32, kind="ExternalInput").ap()
    mem_scale_d = nc.dram_tensor("mem_scale", [H], F32, kind="ExternalInput").ap()
    iter_mix_d = nc.dram_tensor("iter_mix", [1], F32, kind="ExternalInput").ap()
    w_out = nc.dram_tensor("w_out", [D, D], F32, kind="ExternalInput").ap()
    b_out = nc.dram_tensor("b_out", [D], F32, kind="ExternalInput").ap()
    dlog_d = nc.dram_tensor("dlog", [H], F32, kind="ExternalInput").ap()
    w_g = nc.dram_tensor("w_g", [INNER, D], F32, kind="ExternalInput").ap()
    w_u = nc.dram_tensor("w_u", [INNER, D], F32, kind="ExternalInput").ap()
    w_dn = nc.dram_tensor("w_dn", [D, INNER], F32, kind="ExternalInput").ap()

    qkvT_s = nc.dram_tensor("qkvT_s", [128, 3 * 8192], BF16).ap()
    mvT_s = nc.dram_tensor("mvT_s", [128, 8192], BF16).ap()
    outT_s = nc.dram_tensor("outT_s", [128, 8192], BF16).ap()
    wguT_s = nc.dram_tensor("wguT_s", [128, 8 * 8192], FP8).ap()
    wdT_s = nc.dram_tensor("wdT_s", [128, 32 * 1024], FP8).ap()
    tens.update(qkvT_s=qkvT_s, mvT_s=mvT_s, outT_s=outT_s, wguT_s=wguT_s,
                wdT_s=wdT_s)

    PS = nc.alloc_psum_tensor("PS", [128, 4096], F32).ap()
    PSB = PS.bitcast(BF16)

    def bank(i, n=1):
        return PS[:, 512 * i:512 * (i + n)]

    def bankb(i, n=1):
        return PSB[:, 1024 * i:1024 * (i + n)]

    _cur = [0]

    def cat(name, shape, dtype, sz):
        ap = at(name, _cur[0], shape, dtype)
        _cur[0] += (sz + 63) // 64 * 64
        return ap

    ident = cat("ident", [128, 128], BF16, 256)
    identf = cat("identf", [128, 128], F32, 512)
    ident64 = cat("ident64", [128, 64], BF16, 128)
    diagmask = cat("diagmask", [128, 128], BF16, 256)
    strictm = cat("strictm", [128, 128], F32, 512)
    dmats = cat("dmats", [128, 16 * 128], BF16, 4096)
    onesrow = cat("onesrow", [1, 128], BF16, 256)
    onesrow_f = cat("onesrow_f", [1, 128], F32, 512)
    onescol_f = cat("onescol_f", [128, 1], F32, 4)
    blockones = cat("blockones", [128, 2], BF16, 4)
    vcols = cat("vcols", [128, 16], F32, 64)
    scaleg = cat("scaleg", [128, 4 * 128], BF16, 1024)
    dcolC = cat("dcolC", [128, 4], F32, 16)
    alcol = cat("alcol", [128, 1], F32, 4)
    mscbc = cat("mscbc", [128, 16], F32, 64)
    bcolq = cat("bcolq", [128, 24], F32, 96)
    mgbcol = cat("mgbcol", [16, 1], F32, 4)
    wlT = cat("wlT", [128, 8 * 272], BF16, 4352)
    mwcol = cat("mwcol", [128, 4], F32, 16)
    c127 = cat("c127", [128, 1], F32, 4)
    prow = cat("prow", [1, 128], F32, 512)
    pcol = cat("pcol", [128, 1], F32, 4)
    c128 = cat("c128", [128, 1], F32, 4)
    rowa = cat("rowa", [1, 16], F32, 64)
    rowb = cat("rowb", [1, 16], F32, 64)
    rowc = cat("rowc", [1, 16], F32, 64)
    rown = cat("rown", [1, 16], F32, 64)
    arow_b = cat("arow_b", [1, 128], F32, 512)
    brow_b = cat("brow_b", [1, 128], F32, 512)
    s4 = cat("s4", [1, 8], F32, 32)
    lndbc = cat("lndbc", [128, 16], F32, 64)
    snb = cat("snb", [1, 16], F32, 64)
    mgrow = cat("mgrow", [1, 16], F32, 64)
    mgrow_bf = cat("mgrow_bf", [1, 16], BF16, 32)
    fcol = cat("fcol", [128, 8], F32, 32)
    fcol2 = cat("fcol2", [128, 8], F32, 32)
    hpart = cat("hpart", [128, 8], F32, 32)
    f2S = cat("f2S", [128, NT], F32, 64)
    f2Q = cat("f2Q", [128, NT], F32, 64)
    f2R = cat("f2R", [128, NT], F32, 64)
    f2M = cat("f2M", [128, NT], F32, 64)
    statS = cat("statS", [128, NT], F32, 64)
    statQ = cat("statQ", [128, NT], F32, 64)
    statR = cat("statR", [128, NT], F32, 64)
    statM = cat("statM", [128, NT], F32, 64)
    p5col = cat("p5col", [128, 16], F32, 64)
    magiccol = cat("magiccol", [128, 1], F32, 4)
    assert _cur[0] <= 16384, _cur[0]

    CB = 16384
    xa = at("xa", CB, [128, NT * 1024], F32)
    xb = at("xb", CB + 65536, [128, 8 * T], BF16)
    ms_tok = at("ms_tok", CB + 98304, [128, NT * 16], F32)
    gatesig = at("gatesig", CB + 99328, [128, NT * 16], F32)
    gcols = at("gcols", CB + 100352, [128, NT], F32)
    ncols = at("ncols", CB + 100416, [128, NT * 16], F32)
    MEMB = CB + 101440
    TOP = 211008

    mo = MEMB
    w12_tok = at("w12_tok", mo, [128, NT * 128], BF16); mo += 4096
    r12_tok = at("r12_tok", mo, [128, NT * 128], BF16); mo += 4096
    jw_tok = at("jw_tok", mo, [128, NT * 96 + 64], BF16); mo += 3328
    rd_tok = at("rd_tok", mo, [128, NT * 96 + 64], BF16); mo += 3328
    FEATB = mo
    jwT, rdT, rdsT = [], [], []
    for g in range(4):
        jwT.append(at(f"jwT{g}", mo, [128, T], BF16)); mo += 4096
        rdT.append(at(f"rdT{g}", mo, [128, T], BF16)); mo += 4096
        rdsT.append(at(f"rdsT{g}", mo, [128, T], BF16)); mo += 4096
    at_sb = at("at_sb", mo, [128, 2 * 2048], BF16); mo += 8192
    rdm_sb = at("rdm_sb", mo, [128, 4 * 512], BF16); mo += 4096
    m0all = at("m0all", mo, [128, 24], F32); mo += 96
    m0d = at("m0d", mo, [128, 24], F32); mo += 96
    m0bf = at("m0bf", mo, [128, 24], BF16); mo += 64
    jwv_t = at("jwv_t", mo, [128, 24], BF16); mo += 64
    prod1 = at("prod1", mo, [128, 384], F32); mo += 1536
    prod2 = at("prod2", mo, [128, 384], F32); mo += 1536
    sigt = at("sigt", mo, [128, 256], F32); mo += 1024
    sigt2 = at("sigt2", mo, [128, 256], F32); mo += 1024
    padbuf = at("padbuf", mo, [128, 2048], BF16); mo += 4096
    assert mo <= TOP, mo
    lraw = at("lraw", FEATB, [128, NT * 96], F32)
    sall = at("sall", FEATB + 6144, [128, 512], F32)
    rall = at("rall", FEATB + 8192, [128, 512], F32)
    nwt = at("nwt", FEATB + 10240, [128, 512], F32)
    fmst = at("fmst", FEATB + 12288, [128, 2048], BF16)
    fmsh = at("fmsh", FEATB + 16384, [128, 2048], BF16)
    lraw2 = at("lraw2", FEATB + 20480, [128, NT * 96], F32)

    ao = MEMB
    qkvwin = at("qkvwin", ao, [128, 2 * 3 * 1024], BF16); ao += 12288
    qaug, kaug, vaug, ptb = [], [], [], []
    for i in range(2):
        qaug.append(at(f"qaug{i}", ao, [65, T], BF16)); ao += 4096
    for i in range(2):
        kaug.append(at(f"kaug{i}", ao, [65, T], BF16)); ao += 4096
    for i in range(2):
        vaug.append(at(f"vaug{i}", ao, [128, 16 * 66], BF16)); ao += 2112
    vt_st = at("vt_st", ao, [128, T], BF16); ao += 4096
    for i in range(2):
        ptb.append(at(f"ptb{i}", ao, [128, 1024], BF16)); ao += 2048
    oT_st = at("oT_st", ao, [65, T], BF16); ao += 4096
    sq_b = at("sq_b", ao, [128, T], BF16); ao += 4096
    SEQB = ao
    seqout = at("seqout", ao, [128, NT * 1024], BF16); ao += 32768
    assert ao <= TOP, ao

    mvout_w = at("mvout_w", MEMB, [128, 2 * 8192], BF16)
    cmbT = at("cmbT", MEMB + 32768, [128, 1024], BF16)
    ln_scrap = at("ln_scrap", MEMB + 34816, [128, 1024], F32)
    outb_row = at("outb_row", MEMB + 38912, [1, D], BF16)
    mvb_row = at("mvb_row", MEMB + 40960, [1, D], BF16)
    assert MEMB + 43008 <= SEQB

    wguT = at("wguT", CB + 65536, [128, 8 * 8192], FP8)
    wdTr = at("wdTr", CB + 131072, [128, 32 * 1024], FP8)
    fo = CB + 163840
    htile = at("htile", fo, [128, INNER], F32); fo += 16384
    hqb0 = at("hqb0", fo, [128, 1024], BF16); fo += 2048
    silu_t2 = at("silu_t2", fo, [128, 512], F32); fo += 2048
    xq_t = at("xq_t", fo, [128, 1024], BF16); fo += 2048
    xqT = at("xqT", fo, [128, 1024], BF16); fo += 2048
    xqT1 = at("xqT1", fo, [128, 1024], BF16); fo += 2048
    dsl0 = at("dsl0", fo, [128, 1024], BF16); fo += 2048
    silu_t = at("silu_t", fo, [128, 512], F32); fo += 2048
    assert fo <= TOP, fo
    sq_scr = htile[:, 3072:4096]

    prep_stage = at("prep_stage", CB, [128, 32 * 1024], F32)
    prep_w8f = at("prep_w8f", CB + 131072, [128, 32768], FP8)
    prep_w16 = prep_w8f.bitcast(BF16)
    PQ = CB + 163840
    prep_tq = at("prep_tq", PQ, [128, 1024], F32); PQ += 4096
    prep_tq2 = at("prep_tq2", PQ, [128, 1024], F32); PQ += 4096
    prep_tern = at("prep_tern", PQ, [128, 1024], BF16); PQ += 2048
    prep_tern2 = at("prep_tern2", PQ, [128, 1024], BF16); PQ += 2048
    prep_st2 = at("prep_st2", PQ, [128, 2 * 1024], BF16); PQ += 4096
    prep_scl = at("prep_scl", PQ, [128, 1], F32); PQ += 64
    prep_par = at("prep_par", PQ, [128, 4], F32); PQ += 64
    prep_abs = at("prep_abs", PQ, [128, 32], F32); PQ += 128
    brow3k = at("brow3k", PQ, [1, 3072], F32); PQ += 12288
    assert PQ <= TOP, PQ

    for s in ("pe", "ve", "sc", "gp"):
        P.sem(s)
    scratch_toks = []
    out_toks = [[], []]

    tapset = {}
    for tag, name in taps:
        tapset.setdefault(tag, []).append(name)

    def dbg_dump(tag):
        if tag not in tapset:
            return
        P.barrier()
        for name in tapset[tag]:
            ap = tens[name]
            dd = nc.dram_tensor("dbg_" + name, list(ap.shape), ap.dtype,
                                kind="ExternalOutput").ap()
            P.dbg_outs.append("dbg_" + name)
            P.dma("sy", lambda e, dd=dd, ap=ap: e.dma_start(dd, ap),
                  wait=P.dma_all())
        P.barrier()

    def emit_consts():
        dx0 = []
        dx0.append(P.dma("sy", lambda e: e.dma_start(rowa, dlog_d.unsqueeze(0))))
        dx0.append(P.dma("sy", lambda e: e.dma_start(snb, mem_scale_d.unsqueeze(0))))
        dx0.append(P.dma("sy", lambda e: e.dma_start(s4[:, 0:1], iter_mix_d.unsqueeze(0))))
        dx0.append(P.dma("sy", lambda e: e.dma_start(mgrow, b_mg.unsqueeze(0))))
        dx0.append(P.dma("sy", lambda e: e.dma_start(brow3k, b_qkv.unsqueeze(0))))

        def gpc(fn):
            P.gp(fn, wait=[("gp", P.val("gp"))], inc="gp")

        for t_ in (ident, identf, diagmask, strictm, onesrow, onesrow_f,
                   onescol_f, ident64):
            gpc(lambda e, t=t_: e.memset(t, 1.0))
        gpc(lambda e: e.affine_select(ident, ident, [[1, 128]], OP.is_equal,
                                      0.0, base=0, channel_multiplier=-1))
        gpc(lambda e: e.affine_select(identf, identf, [[1, 128]], OP.is_equal,
                                      0.0, base=0, channel_multiplier=-1))
        gpc(lambda e: e.affine_select(ident64, ident64, [[1, 64]], OP.is_equal,
                                      0.0, base=64, channel_multiplier=-1))
        gpc(lambda e: e.affine_select(diagmask, diagmask, [[1, 128]], OP.is_ge,
                                      0.0, base=0, channel_multiplier=-1))
        gpc(lambda e: e.affine_select(strictm, strictm, [[1, 128]], OP.is_ge,
                                      0.0, base=-1, channel_multiplier=-1))
        gpc(lambda e: e.memset(blockones, 0.0))
        gpc(lambda e: e.memset(blockones[0:64, 0:1], 1.0))
        gpc(lambda e: e.memset(blockones[64:128, 1:2], 1.0))
        gpc(lambda e: e.iota(prow, [[1, 128]], base=0, channel_multiplier=0,
                             allow_small_or_imprecise_dtypes=True))
        gpc(lambda e: e.iota(pcol, [[0, 1]], base=0, channel_multiplier=1,
                             allow_small_or_imprecise_dtypes=True))
        gpc(lambda e: e.iota(c128, [[0, 1]], base=128, channel_multiplier=-1,
                             allow_small_or_imprecise_dtypes=True))
        gpc(lambda e: e.memset(magiccol, MAGIC))
        gpc(lambda e: e.memset(mwcol, 0.0))
        gpc(lambda e: e.memset(c127, 127.0))
        gp0 = P.val("gp")

        P.sc(lambda e: e.activation(rowb, rowa, AF.Exp, bias=0.0, scale=-1.0),
             wait=dx0 + [("gp", gp0)], inc="sc")
        P.ve(lambda e: e.tensor_scalar(rowb, rowb, 1.0, 0.0, OP.add, OP.add),
             wait=[("sc", P.val("sc"))])
        P.ve(lambda e: e.reciprocal(rowa, rowb), inc="ve")
        P.sc(lambda e: e.activation(rowb, rowa, AF.Ln, bias=0.0, scale=1.0),
             wait=[("ve", P.val("ve"))], inc="sc")
        sc_ln = P.val("sc")
        P.ve(lambda e: e.tensor_scalar(rown, rowb, -1.0, 0.0, OP.mult, OP.add),
             wait=[("sc", sc_ln)], inc="ve")
        P.sc(lambda e: e.activation(rowc, rowb, AF.Exp, bias=0.0, scale=128.0),
             wait=[("sc", sc_ln)], inc="sc")
        sc_dc = P.val("sc")
        P.sc(lambda e: e.activation(s4[:, 1:2], s4[:, 0:1], AF.Exp, bias=0.0,
                                    scale=-1.0), wait=dx0)
        P.ve(lambda e: e.tensor_scalar(s4[:, 1:2], s4[:, 1:2], 1.0, 0.0,
                                       OP.add, OP.add), wait=[("sc", P.val("sc"))])
        P.ve(lambda e: e.reciprocal(s4[:, 1:2], s4[:, 1:2]), inc="ve")
        ve_al = P.val("ve")
        P.ve(lambda e: e.tensor_copy(mgrow_bf, mgrow), wait=dx0, inc="ve")

        def bcast(dst, src_row, n, wait):
            P.pe(lambda e, dst=dst, src=src_row, n=n: e.matmul(
                bank(7)[:, 0:n], onesrow_f, src, start=True, stop=True),
                wait=wait, inc="pe")
            P.ve(lambda e, dst=dst, n=n: e.tensor_copy(dst, bank(7)[:, 0:n]),
                 wait=[("pe", P.val("pe"))], inc="ve")
            return [("ve", P.val("ve"))]

        wv = bcast(lndbc, rowb, 16, [("sc", sc_ln), ("ve", P.val("ve")), ("gp", gp0)])
        wv = bcast(alcol, s4[:, 1:2], 1, wv + [("ve", ve_al)])
        wv = bcast(mscbc, snb, 16, wv + dx0)
        for h in range(H):
            P.sc(lambda e, h=h: e.activation(vcols[:, h:h + 1], c128, AF.Exp,
                                             bias=0.0, scale=lndbc[:, h:h + 1]),
                 wait=wv + [("gp", gp0)] if h == 0 else ())
        pass

        for g in range(4):
            for j in range(4):
                h = 4 * g + j
                P.pe(lambda e, g=g, j=j, h=h: e.matmul(
                    bank(7)[32 * j:32 * (j + 1), 64 + g:65 + g],
                    onesrow_f[:, 0:32], rowc[:, h:h + 1],
                    start=True, stop=True,
                    tile_position=(0, 32 * j)),
                    wait=wv + [("sc", sc_dc)] if (g == 0 and j == 0) else ())
        P.mark("pe")
        P.ve(lambda e: e.tensor_copy(dcolC, bank(7)[:, 64:68]),
             wait=[("pe", P.val("pe"))], inc="ve")

        P.ve(lambda e: e.memset(bank(5)[:, 0:128], 0.0),
             wait=[("pe", P.val("pe"))], inc="ve")
        for h in range(H):
            g, j = h // 4, h % 4
            P.sc(lambda e, h=h: e.activation(arow_b, prow, AF.Exp, bias=0.0,
                                             scale=rowb[:, h:h + 1]),
                 wait=[("ve", P.val("ve")), ("pe", P.val("pe"))])
            P.sc(lambda e, h=h: e.activation(brow_b, prow, AF.Exp, bias=0.0,
                                             scale=rown[:, h:h + 1]), inc="sc")
            sc_ab = P.val("sc")
            P.pe(lambda e: e.matmul(bank(6)[:, 0:128], brow_b, arow_b,
                                    start=True, stop=True),
                 wait=[("sc", sc_ab)], inc="pe")
            P.ve(lambda e, h=h: e.tensor_tensor(
                dmats[:, 128 * h:128 * (h + 1)], bank(6)[:, 0:128], strictm,
                OP.mult), wait=[("pe", P.val("pe"))], inc="ve")
            P.pe(lambda e, j=j: e.matmul(bank(5)[32 * j:32 * j + 6, 0:128],
                                         onesrow_f[:, 0:6], arow_b,
                                         start=True, stop=True,
                                         tile_position=(0, 32 * j)),
                 wait=[("sc", sc_ab)], inc="pe")
            if j == 3:
                P.ve(lambda e, g=g: e.tensor_copy(
                    scaleg[:, 128 * g:128 * (g + 1)], bank(5)[:, 0:128]),
                    wait=[("pe", P.val("pe"))], inc="ve")

        for oc in range(24):
            P.pe(lambda e, oc=oc: e.transpose(
                bank(4)[:, oc:oc + 1], brow3k[:, 128 * oc:128 * (oc + 1)],
                identf[0:1, 0:1]),
                wait=dx0 + [("ve", P.val("ve"))] if oc == 0 else ())
        P.mark("pe")
        P.ve(lambda e: e.tensor_copy(bcolq, bank(4)[:, 0:24]),
             wait=[("pe", P.val("pe"))])
        P.ve(lambda e: e.tensor_scalar(bcolq[:, 0:8], bcolq[:, 0:8], QKV_SCALE,
                                       0.0, OP.mult, OP.add), inc="ve")
        P.pe(lambda e: e.transpose(bank(4)[0:16, 32:33], mgrow, identf[0:1, 0:1]),
             wait=[("ve", P.val("ve"))], inc="pe")
        P.ve(lambda e: e.tensor_copy(mgbcol, bank(4)[0:16, 32:33]),
             wait=[("pe", P.val("pe"))], inc="ve")
        P.barrier()

    def emit_prep_small():
        wlT3 = wlT.rearrange("p (dc f) -> p dc f", f=272)
        dwv = []
        dwv.append(P.dma("gp", lambda e: e.dma_start(prep_st2[0:64, 0:1024], w_wl["w1"])))
        dwv.append(P.dma("gp", lambda e: e.dma_start(prep_st2[64:128, 0:1024], w_wl["w2"])))
        dwv.append(P.dma("gp", lambda e: e.dma_start(prep_st2[0:64, 1024:2048], w_wl["r1"])))
        dwv.append(P.dma("gp", lambda e: e.dma_start(prep_st2[64:128, 1024:2048], w_wl["r2"])))
        for sect in range(2):
            for dc in range(8):
                P.pe(lambda e, dc=dc, sect=sect: e.transpose(
                    bankb(sect)[:, 128 * dc:128 * (dc + 1)],
                    prep_st2[:, 1024 * sect + 128 * dc:1024 * sect + 128 * (dc + 1)],
                    ident), wait=dwv if dc == 0 else ())
            P.mark("pe")
            P.ve(lambda e, sect=sect: e.tensor_copy(
                wlT3[:, :, 128 * sect:128 * (sect + 1)],
                bankb(sect).rearrange("p (dc f) -> p dc f", f=128)),
                wait=[("pe", P.val("pe"))], inc="ve")
        dwv = [P.dma("gp", lambda e: e.dma_start(prep_st2[0:16, 0:1024], w_mg),
                     wait=[("ve", P.val("ve"))])]
        for dc in range(8):
            P.pe(lambda e, dc=dc: e.transpose(
                bankb(2)[:, 16 * dc:16 * (dc + 1)],
                prep_st2[0:16, 128 * dc:128 * (dc + 1)], ident[0:16, 0:16]),
                wait=dwv if dc == 0 else ())
        P.mark("pe")
        P.ve(lambda e: e.tensor_copy(
            wlT3[:, :, 256:272],
            bankb(2).rearrange("p (dc f) -> p dc f", f=16)[:, 0:8, :]),
            wait=[("pe", P.val("pe"))], inc="ve")

    def emit_prep_big():
        secs = [(w_qkv, 0, True, qkvT_s, 0), (w_qkv, 8, False, qkvT_s, 8192),
                (w_qkv, 16, False, qkvT_s, 16384),
                (w_mv, 0, False, mvT_s, 0), (w_out, 0, False, outT_s, 0)]
        last_tr = {0: P.val("pe"), 1: P.val("pe")}
        last_ev = {0: P.val("ve"), 1: P.val("ve")}
        w16_free = []
        for dram, rc0, isq, scr, scol in secs:
            for rc in range(8):
                buf = rc % 2
                dst = prep_st2[:, 1024 * buf:1024 * (buf + 1)]
                dwv = [P.dma("gp", lambda e, d=dram, r=rc0 + rc, dst=dst:
                             e.dma_start(dst, d[128 * r:128 * (r + 1), :]),
                             wait=[("pe", last_tr[buf])])]
                bb = rc % 2
                for dc in range(8):
                    P.pe(lambda e, dc=dc, bb=bb, dst=dst: e.transpose(
                        bankb(bb)[:, 128 * dc:128 * (dc + 1)],
                        dst[:, 128 * dc:128 * (dc + 1)], ident),
                        wait=dwv + [("ve", last_ev[bb])] if dc == 0 else ())
                P.mark("pe")
                last_tr[buf] = P.val("pe")
                pev = P.val("pe")
                if scr is qkvT_s:
                    dsl = prep_w16[:, 1024 * rc:1024 * (rc + 1)]
                    src = bankb(bb)
                else:
                    dsl = prep_w16[:, 0:8192].rearrange(
                        "p (dc f) -> p dc f", f=1024)[:, :, 128 * rc:128 * (rc + 1)]
                    src = bankb(bb).rearrange("p (dc f) -> p dc f", f=128)
                wt = [("pe", pev)] + (w16_free if rc == 0 else [])
                if isq:
                    P.ve(lambda e, dsl=dsl, src=src: e.tensor_scalar(
                        dsl, src, QKV_SCALE, 0.0, OP.mult, OP.add),
                        wait=wt, inc="ve")
                else:
                    P.ve(lambda e, dsl=dsl, src=src: e.tensor_copy(dsl, src),
                         wait=wt, inc="ve")
                last_ev[bb] = P.val("ve")
            vev = P.val("ve")
            w16_free = [P.dma("sy", lambda e, scr=scr, scol=scol: e.dma_start(
                scr[:, scol:scol + 8192], prep_w16[:, 0:8192]),
                wait=[("ve", vev)])]
            scratch_toks.extend(w16_free)

    PREP_SPECS = [(w_g, 32, 1024, "g"), (w_u, 32, 1024, "u"),
                  (w_dn, 8, 4096, "d")]
    prep_stage_free = {k: [] for k in range(4)}
    prep_dma_toks = {}

    def emit_prep_ffn_dma(kind):
        dram, nrc, ccols, _ = next(s for s in PREP_SPECS if s[3] == kind)
        rpc = nrc // 8
        toks = {k: [] for k in range(4)}
        for hk in range(8):
            k = hk // 2
            dst = prep_stage[:, 4096 * hk:4096 * (hk + 1)].rearrange(
                "p (r c) -> p r c", c=ccols)
            srcd = dram[128 * rpc * hk:128 * rpc * (hk + 1), :].rearrange(
                "(r p) c -> p r c", p=128)
            toks[k].append(P.dma_big(hk, lambda e, dst=dst, srcd=srcd:
                                     e.dma_start(dst, srcd),
                                     wait=prep_stage_free[k]))
        prep_dma_toks[kind] = toks

    def emit_prep_ffn():
        w8_free = list(scratch_toks)
        for wx, (dram, nrc, ccols, kind) in enumerate(PREP_SPECS):
            dwv = prep_dma_toks[kind]
            for k in range(4):
                P.ve(lambda e, k=k: e.reduce_sum(
                    prep_par[:, k:k + 1],
                    prep_stage[:, 8192 * k:8192 * (k + 1)], axis=AX,
                    apply_absolute_value=True),
                    wait=dwv[k], inc=("ve" if k == 3 else None))
            P.ve(lambda e: e.tensor_reduce(prep_scl, prep_par, AX, OP.add))
            P.pe(lambda e: e.matmul(bank(4)[0:1, 0:1], prep_scl, onescol_f,
                                    start=True, stop=True),
                 wait=[("ve", P.val("ve"))], inc="pe")
            P.ve(lambda e: e.tensor_scalar(s4[:, 0:1], bank(4)[0:1, 0:1],
                                           1024.0 / (1 << 22), 0.0,
                                           OP.mult, OP.add),
                 wait=[("pe", P.val("pe"))], inc="ve")
            P.pe(lambda e: e.matmul(bank(4)[:, 3:4], onesrow_f, s4[:, 0:1],
                                    start=True, stop=True),
                 wait=[("ve", P.val("ve"))], inc="pe")
            P.ve(lambda e: e.tensor_copy(prep_scl, bank(4)[:, 3:4]),
                 wait=[("pe", P.val("pe"))], inc="ve")
            m1tok = P.val("ve")
            for blk in range(32):
                P.sc(lambda e, blk=blk: e.activation(
                    prep_tq, prep_stage[:, 1024 * blk:1024 * (blk + 1)],
                    AF.Abs, bias=0.0, scale=1.0,
                    accum_out=prep_abs[:, blk:blk + 1]),
                    wait=([("ve", m1tok)] if blk == 0 else ()), inc="sc")
            P.ve(lambda e: e.tensor_tensor(
                prep_abs, prep_abs,
                bass.AP(prep_scl.tensor, prep_scl.offset,
                        [list(prep_scl.ap[0]), [0, 32]]), OP.subtract),
                 wait=[("sc", P.val("sc"))])
            P.ve(lambda e: e.tensor_reduce(prep_par[:, 0:1], prep_abs, AX,
                                           OP.add), inc="ve")
            P.pe(lambda e: e.matmul(bank(4)[0:1, 4:5], prep_par[:, 0:1],
                                    onescol_f, start=True, stop=True),
                 wait=[("ve", P.val("ve"))], inc="pe")
            P.ve(lambda e: e.tensor_copy(s4[:, 1:2], bank(4)[0:1, 4:5]),
                 wait=[("pe", P.val("pe"))])
            P.ve(lambda e: e.tensor_tensor(s4[:, 2:3], s4[:, 1:2],
                                           bank(4)[0:1, 0:1], OP.add),
                 inc="ve")
            P.ve(lambda e, n=nrc, c=ccols: e.tensor_scalar(
                s4[:, 4:5], s4[:, 2:3], 1.0 / (n * 128 * c), 1e-5,
                OP.mult, OP.max))
            P.ve(lambda e: e.reciprocal(s4[:, 5:6], s4[:, 4:5]), inc="ve")
            P.pe(lambda e: e.matmul(bank(4)[:, 1:2], onesrow_f, s4[:, 5:6],
                                    start=True, stop=True),
                 wait=[("ve", P.val("ve"))])
            P.pe(lambda e: e.matmul(bank(4)[:, 2:3], onesrow_f, s4[:, 4:5],
                                    start=True, stop=True), inc="pe")
            wi = {"g": 0, "u": 1, "d": 2}[kind]
            P.ve(lambda e: e.tensor_copy(prep_scl, bank(4)[:, 1:2]),
                 wait=[("pe", P.val("pe"))])
            P.ve(lambda e, wi=wi: e.tensor_copy(mwcol[:, wi:wi + 1],
                                                bank(4)[:, 2:3]), inc="ve")
            vescale = P.val("ve")

            npieces = nrc * ccols // 1024
            last_tr = {0: P.val("pe"), 1: P.val("pe")}
            last_ev = {2: ("ve", P.val("ve")), 3: ("ve", P.val("ve"))}
            terns = [prep_tern, prep_tern2]
            tqs = [prep_tq, prep_tq2]
            tq_free = {0: P.val("ve"), 1: P.val("ve")}
            pend_ev = [None]
            stage_done = {}
            for piece in range(npieces):
                src = prep_stage[:, 1024 * piece:1024 * (piece + 1)]
                trn = terns[piece % 2]
                tq = tqs[piece % 2]
                P.sc(lambda e, src=src, tq=tq: e.activation(
                    tq, src, AF.Identity, bias=magiccol, scale=prep_scl),
                    wait=([("ve", vescale)] if piece == 0 else
                          [("ve", tq_free[piece % 2])]), inc="sc")
                scv = P.val("sc")
                if piece % 8 == 7:
                    stage_done[piece // 8] = scv
                P.ve(lambda e, tq=tq: e.tensor_scalar(
                    tq, tq, MAGIC + 1.0, -MAGIC, OP.min, OP.add),
                     wait=[("sc", scv)])
                P.ve(lambda e, trn=trn, tq=tq: e.tensor_scalar(
                    trn, tq, -1.0, 1.0, OP.max, OP.mult),
                     wait=[("pe", last_tr[piece % 2])], inc="ve")
                tq_free[piece % 2] = P.val("ve")
                if pend_ev[0] is not None:
                    pend_ev[0]()
                    pend_ev[0] = None
                bb = 2 + piece % 2
                for dc in range(8):
                    P.pe(lambda e, dc=dc, bb=bb, trn=trn: e.transpose(
                        bankb(bb)[:, 128 * dc:128 * (dc + 1)],
                        trn[:, 128 * dc:128 * (dc + 1)], ident),
                        wait=[("ve", tq_free[piece % 2]),
                              last_ev[bb]] if dc == 0 else ())
                P.mark("pe")
                last_tr[piece % 2] = P.val("pe")
                pev = P.val("pe")
                if kind in ("g", "u"):
                    rc = piece
                    dsl = prep_w8f.rearrange("p (dc f) -> p dc f", f=4096)[
                        :, :, 128 * rc:128 * (rc + 1)]
                else:
                    rc, icq = piece // 4, piece % 4
                    dsl = prep_w8f.rearrange("p (icq ic f) -> p icq ic f",
                                             icq=4, ic=8)[
                        :, icq, :, 128 * rc:128 * (rc + 1)]
                wt = [("pe", pev)] + (w8_free if piece == 0 else [])

                def mk_ev(bb=bb, dsl=dsl, wt=wt):
                    def em():
                        P.ve(lambda e, bb=bb, dsl=dsl: e.tensor_copy(
                            dsl, bankb(bb).rearrange("p (c f) -> p c f",
                                                     f=128)),
                            wait=wt, inc="ve")
                        last_ev[bb] = ("ve", P.val("ve"))
                    return em
                pend_ev[0] = mk_ev()
            if pend_ev[0] is not None:
                pend_ev[0]()
                pend_ev[0] = None
            gev = P.val("ve")
            if kind == "g":
                tok = P.dma("sy", lambda e: e.dma_start(
                    wguT_s.rearrange("p (dc f) -> p dc f", f=8192)[:, :, 0:4096],
                    prep_w8f.rearrange("p (dc f) -> p dc f", f=4096)),
                    wait=[("ve", gev)])
            elif kind == "u":
                tok = P.dma("sy", lambda e: e.dma_start(
                    wguT_s.rearrange("p (dc f) -> p dc f", f=8192)[:, :, 4096:8192],
                    prep_w8f.rearrange("p (dc f) -> p dc f", f=4096)),
                    wait=[("ve", gev)])
            else:
                tok = P.dma("sy", lambda e: e.dma_start(wdT_s, prep_w8f),
                            wait=[("ve", gev)])
            w8_free = [tok]
            scratch_toks.append(tok)
            for k in range(4):
                prep_stage_free[k] = [("sc", stage_done[k])]
            if wx + 1 < len(PREP_SPECS):
                emit_prep_ffn_dma(PREP_SPECS[wx + 1][3])

    def emit_p0(b):
        dxv = [P.dma("sy", lambda e, b=b: e.dma_start(
            xa.rearrange("p (t d) -> p t d", d=1024),
            x_d[b].rearrange("(t q) d -> q t d", q=128)),
            wait=P.dma_all() if b else [])]
        ev_guard = {0: P.val("ve"), 1: P.val("ve")}
        for dc in range(8):
            for tq in range(4):
                bk = (dc * 4 + tq) % 2
                for ti in range(4):
                    t = tq * 4 + ti
                    w = dxv + [("ve", ev_guard[bk])] if ti == 0 else ()
                    P.pe(lambda e, dc=dc, t=t, bk=bk, ti=ti: e.transpose(
                        bank(bk)[:, 128 * ti:128 * (ti + 1)],
                        xa[:, 1024 * t + 128 * dc:1024 * t + 128 * (dc + 1)],
                        identf), wait=w)
                P.mark("pe")
                pev = P.val("pe")
                P.ve(lambda e, dc=dc, tq=tq, bk=bk: e.tensor_copy(
                    xb[:, 2048 * dc + 512 * tq:2048 * dc + 512 * (tq + 1)],
                    bank(bk)), wait=[("pe", pev)], inc="ve")
                ev_guard[bk] = P.val("ve")

    PAIRS = [(0, 1), (0, 2), (0, 3), (1, 2), (1, 3), (2, 3)]

    def wsl(tok, half, i):
        return bass.AP(tok.tensor, tok.offset + 64 * half + i,
                       [list(tok.ap[0]), [128, NT], [4, 16]])

    def lsl(k):
        return bass.AP(lraw.tensor, lraw.offset + k,
                       [list(lraw.ap[0]), [96, NT], [6, 16]])

    def ssl(soff):
        return bass.AP(sall.tensor, sall.offset + soff,
                       [list(sall.ap[0]), [16, NT], [1, 16]])

    def l2c():
        return bass.AP(lraw2.tensor, lraw2.offset,
                       [list(lraw2.ap[0]), [16, NT], [1, 16]])

    def jsl(dst, c0, cnt, cstep):
        return bass.AP(dst.tensor, dst.offset + c0,
                       [list(dst.ap[0]), [96, NT], [6, 16], [cstep, cnt]])

    def lsl2(k0, cnt, kstep):
        return bass.AP(lraw.tensor, lraw.offset + k0,
                       [list(lraw.ap[0]), [96, NT], [6, 16], [kstep, cnt]])

    def rbc(soff, cnt):
        return bass.AP(rall.tensor, rall.offset + soff,
                       [list(rall.ap[0]), [16, NT], [1, 16], [0, cnt]])

    def emit_p2():
        for sect, tok_dst in ((0, w12_tok), (1, r12_tok)):
            for nb in range(4):
                for dc in range(8):
                    w = [("ve", P.val("ve"))] if (nb == 0 and dc == 0) else ()
                    P.pe(lambda e, nb=nb, dc=dc, sect=sect: e.matmul(
                        bank(nb),
                        wlT[:, 272 * dc + 128 * sect:272 * dc + 128 * (sect + 1)],
                        xb[:, 2048 * dc + 512 * nb:2048 * dc + 512 * (nb + 1)],
                        start=(dc == 0), stop=(dc == 7)), wait=w)
            P.mark("pe")
            pev = P.val("pe")
            P.ve(lambda e: e.tensor_copy(fmst, PS[:, 0:2048]),
                 wait=[("pe", pev)], inc="ve")
            if sect == 0:
                P.ve(lambda e: e.tensor_copy(fmsh[0:64, 1:2048],
                                             fmst[0:64, 0:2047]))
                P.ve(lambda e: e.memset(fmsh[0:64, 0:1], 0.0))
                P.ve(lambda e: e.tensor_copy(fmst[0:64, :], fmsh[0:64, :]),
                     inc="ve")
            vev = P.val("ve")
            evg = {4: P.val("ve"), 5: P.val("ve")}
            for tq in range(4):
                bb = 4 + tq % 2
                for ti in range(4):
                    t = tq * 4 + ti
                    w = [("ve", max(vev, evg[bb]))] if ti == 0 else ()
                    P.pe(lambda e, t=t, ti=ti, bb=bb: e.transpose(
                        bankb(bb)[:, 256 * ti:256 * ti + 128],
                        fmst[:, 128 * t:128 * (t + 1)], ident), wait=w)
                P.mark("pe")
                pev = P.val("pe")
                P.ve(lambda e, tq=tq, bb=bb, tok_dst=tok_dst: e.tensor_copy(
                    tok_dst.rearrange("p (t f) -> p t f", f=128)[
                        :, 4 * tq:4 * (tq + 1), :],
                    bankb(bb).rearrange("p (t f) -> p t f", f=256)[:, :, 0:128]),
                    wait=[("pe", pev)], inc="ve")
                evg[bb] = P.val("ve")

        vev = P.val("ve")
        for t in range(NT):
            for dc in range(8):
                P.pe(lambda e, t=t, dc=dc: e.matmul(
                    bank(6)[:, 16 * t:16 * (t + 1)],
                    xb[:, 2048 * dc + 128 * t:2048 * dc + 128 * (t + 1)],
                    wlT[:, 272 * dc + 256:272 * dc + 272],
                    start=(dc == 0), stop=False),
                    wait=([("ve", vev)] if (t == 0 and dc == 0) else ()))
            P.pe(lambda e, t=t: e.matmul(
                bank(6)[:, 16 * t:16 * (t + 1)], onesrow, mgrow_bf,
                start=False, stop=True))
        P.mark("pe")
        pev = P.val("pe")
        P.sc(lambda e: e.activation(gatesig, bank(6)[:, 0:256], AF.Exp,
                                    bias=0.0, scale=-1.0),
             wait=[("pe", pev)], inc="sc")
        P.ve(lambda e: e.tensor_scalar(gatesig, gatesig, 1.0, 0.0, OP.add,
                                       OP.add), wait=[("sc", P.val("sc"))])
        P.ve(lambda e: e.reciprocal(gatesig, gatesig), inc="ve")

        P.ve(lambda e: e.memset(jw_tok, 0.0))
        P.ve(lambda e: e.memset(rd_tok, 0.0), inc="ve")
        for kindi, (tok, dst_tok) in enumerate(((w12_tok, jw_tok),
                                                (r12_tok, rd_tok))):
            for k, (i, j) in enumerate(PAIRS):
                P.ve(lambda e, tok=tok, i=i, j=j, k=k: e.tensor_tensor(
                    lsl(k), wsl(tok, 0, i), wsl(tok, 1, j), OP.mult))
                P.ve(lambda e, tok=tok, i=i, j=j: e.tensor_tensor(
                    l2c(), wsl(tok, 0, j), wsl(tok, 1, i), OP.mult))
                P.ve(lambda e, k=k: e.tensor_tensor(
                    lsl(k), lsl(k), l2c(), OP.subtract))
            soff = 256 * kindi
            P.ve(lambda e, soff=soff: e.tensor_tensor(
                ssl(soff), lsl(0), lsl(0), OP.mult))
            for k in range(1, 6):
                P.ve(lambda e, k=k: e.tensor_tensor(l2c(), lsl(k), lsl(k),
                                                    OP.mult))
                P.ve(lambda e, soff=soff: e.tensor_tensor(
                    sall[:, soff:soff + 256], sall[:, soff:soff + 256],
                    lraw2[:, 0:256], OP.add))
            P.ve(lambda e, soff=soff: e.tensor_single_scalar(
                sall[:, soff:soff + 256], sall[:, soff:soff + 256], 1e-24,
                OP.max), inc="ve")
            P.sc(lambda e, soff=soff: e.activation(
                nwt[:, soff:soff + 256], sall[:, soff:soff + 256], AF.Ln,
                bias=0.0, scale=1.0), wait=[("ve", P.val("ve"))], inc="sc")
            P.sc(lambda e, soff=soff: e.activation(
                rall[:, soff:soff + 256], nwt[:, soff:soff + 256], AF.Exp,
                bias=0.0, scale=-0.5), inc="sc")
            scv = P.val("sc")
            if kindi == 0:
                P.ve(lambda e, d=dst_tok: e.scalar_tensor_tensor(
                    jsl(d, 2, 2, 1), lsl2(3, 2, -1), 1.0, rbc(0, 2),
                    OP.mult, OP.mult), wait=[("sc", scv)])
                P.ve(lambda e, d=dst_tok: e.scalar_tensor_tensor(
                    jsl(d, 0, 2, 5), lsl2(5, 2, -5), 1.0, rbc(0, 2),
                    OP.mult, OP.mult))
                P.ve(lambda e, d=dst_tok: e.scalar_tensor_tensor(
                    jsl(d, 1, 2, 3), lsl2(4, 2, -3), -1.0, rbc(0, 2),
                    OP.mult, OP.mult), inc="ve")
            else:
                P.ve(lambda e, d=dst_tok: e.scalar_tensor_tensor(
                    jsl(d, 0, 6, 1), lsl2(0, 6, 1), 1.0, rbc(256, 6),
                    OP.mult, OP.mult), wait=[("sc", scv)], inc="ve")

    def emit_p3():
        ev_bb = {0: P.val("ve"), 1: P.val("ve")}
        P.ve(lambda e: e.memset(padbuf, 0.0), inc="ve")
        pad_rd = P.val("pe")
        for kind, (tok, dstl) in enumerate(((jw_tok, jwT), (rd_tok, rdT))):
            for cq in range(4):
                for ci in range(4):
                    c = cq * 4 + ci
                    dstp = bass.AP(padbuf.tensor, padbuf.offset + 512 * ci,
                                   [list(padbuf.ap[0]), [32, 16], [1, 6]])
                    srcp = bass.AP(tok.tensor, tok.offset + 96 * c,
                                   [list(tok.ap[0]), [6, 16], [1, 6]])
                    P.ve(lambda e, dstp=dstp, srcp=srcp: e.tensor_copy(
                        dstp, srcp),
                        wait=([("pe", pad_rd)] if ci == 0 else ()),
                        inc=("ve" if ci == 3 else None))
                padtok = P.val("ve")
                for g in range(4):
                    bb = g % 2
                    for ci in range(4):
                        w = []
                        if ci == 0:
                            w = [("ve", padtok), ("ve", ev_bb[bb])]
                        P.pe(lambda e, g=g, ci=ci, bb=bb: e.transpose(
                            bankb(bb)[:, 128 * ci:128 * (ci + 1)],
                            padbuf[:, 512 * ci + 128 * g:512 * ci + 128 * (g + 1)],
                            ident), wait=w)
                    P.mark("pe")
                    pev = P.val("pe")
                    P.ve(lambda e, g=g, cq=cq, bb=bb, dstl=dstl: e.tensor_copy(
                        dstl[g][:, 512 * cq:512 * (cq + 1)], bankb(bb)[:, 0:512]),
                        wait=[("pe", pev)], inc="ve")
                    ev_bb[bb] = P.val("ve")
                pad_rd = P.val("pe")
        for g in range(4):
            src2 = bass.AP(scaleg.tensor, scaleg.offset + 128 * g,
                           [list(scaleg.ap[0]), [0, NT], [1, 128]])
            P.ve(lambda e, g=g, src2=src2: e.tensor_tensor(
                rdsT[g].rearrange("p (c f) -> p c f", f=128),
                rdT[g].rearrange("p (c f) -> p c f", f=128),
                src2, OP.mult))
        P.ve(lambda e: None, inc="ve")
        vev_feat = P.val("ve")
        P.ve(lambda e: e.memset(m0all, 0.0))
        P.ve(lambda e: e.memset(m0bf, 0.0))
        P.ve(lambda e: e.memset(bank(4, 2), 0.0))
        P.ve(lambda e: e.memset(bank(6), 0.0), inc="ve")
        ve_init = P.val("ve")
        m0_ve = {g: ve_init for g in range(4)}
        a_free = {0: vev_feat, 1: vev_feat}
        a_slot = {0: P.val("pe"), 1: P.val("pe")}
        rdm_free = {0: ve_init, 1: ve_init}
        last_upd = [P.val("pe")]
        aev = {}
        rdm_ev = {}
        for cb in range(4):
            for g in range(4):
                par = g % 2
                w0 = [("ve", a_free[0])]
                for cc in range(4):
                    c = cb * 4 + cc
                    for j in range(4):
                        P.pe(lambda e, g=g, c=c, j=j, cc=cc: e.matmul(
                                 PS[:, 512 * j + 128 * cc:512 * j + 128 * (cc + 1)],
                                 jwT[g][32 * j:32 * j + 6, 128 * c:128 * (c + 1)],
                                 rdT[g][32 * j:32 * j + 6, 128 * c:128 * (c + 1)],
                                 start=True, stop=True, tile_position=(32 * j, 0)),
                             wait=w0 if (cc == 0 and j == 0) else ())
                P.mark("pe")
                mm1tok = P.val("pe")
                dm = bass.AP(dmats.tensor, dmats.offset + 128 * 4 * g,
                             [list(dmats.ap[0]), [128, 4], [0, 4], [1, 128]])
                src = bass.AP(PS.tensor, PS.offset,
                              [list(PS.ap[0]), [512, 4], [128, 4], [1, 128]])
                dst = bass.AP(at_sb.tensor, at_sb.offset + 2048 * par,
                              [list(at_sb.ap[0]), [512, 4], [128, 4], [1, 128]])
                P.ve(lambda e, dst=dst, src=src, dm=dm: e.tensor_tensor(
                    dst, src, dm, OP.mult),
                    wait=[("pe", mm1tok), ("pe", a_slot[par])], inc="ve")
                a_free[0] = P.val("ve")
                aev[g] = P.val("ve")
                rb = 4 + par
                for cc in range(4):
                    c = cb * 4 + cc
                    for j in range(4):
                        h = 4 * g + j
                        P.ve(lambda e, g=g, c=c, j=j, h=h: e.tensor_scalar(
                            jwv_t[:, 6 * j:6 * (j + 1)],
                            jw_tok[:, 96 * c + 24 * g + 6 * j:
                                   96 * c + 24 * g + 6 * (j + 1)],
                            vcols[:, h:h + 1], 0.0, OP.mult, OP.add),
                            wait=([("pe", last_upd[0])] if j == 0 else ()),
                            inc=("ve" if j == 3 else None))
                    jwvtok = P.val("ve")
                    P.ve(lambda e, g=g: e.tensor_scalar(
                        m0d[:, 6 * g:6 * (g + 1)], m0all[:, 6 * g:6 * (g + 1)],
                        dcolC[:, g:g + 1], 0.0, OP.mult, OP.add), inc="ve")
                    m0dtok = P.val("ve")
                    for j in range(4):
                        P.pe(lambda e, g=g, c=c, j=j, cc=cc, rb=rb: e.matmul(
                            bank(rb)[:, cc * 128 + 32 * j:cc * 128 + 32 * j + 6],
                            at_sb[:, 2048 * (g % 2) + 512 * j + 128 * cc:
                                  2048 * (g % 2) + 512 * j + 128 * (cc + 1)],
                            jw_tok[:, 96 * c + 24 * g + 6 * j:
                                   96 * c + 24 * g + 6 * (j + 1)],
                            start=True, stop=False),
                            wait=([("ve", aev[g]), ("ve", rdm_free[par])]
                                  if (cc == 0 and j == 0) else ()))
                        P.pe(lambda e, g=g, c=c, j=j, cc=cc, rb=rb: e.matmul(
                            bank(rb)[:, cc * 128 + 32 * j:cc * 128 + 32 * j + 6],
                            rdsT[g][32 * j:32 * j + 6, 128 * c:128 * (c + 1)],
                            m0bf[32 * j:32 * j + 6, 6 * g:6 * (g + 1)],
                            start=False, stop=True,
                            tile_position=(32 * j, 0)),
                            wait=([("ve", m0_ve[g])] if j == 0 else ()))
                    for j in range(4):
                        P.pe(lambda e, g=g, c=c, j=j, cc=cc: e.matmul(
                            bank(6)[32 * j:32 * j + 6,
                                    128 * g + 32 * (cc % 2):128 * g + 32 * (cc % 2) + 6],
                            jwv_t[:, 6 * j:6 * (j + 1)],
                            jw_tok[:, 96 * c + 24 * g + 6 * j:
                                   96 * c + 24 * g + 6 * (j + 1)],
                            start=True, stop=True,
                            tile_position=(0, 32 * j)),
                            wait=([("ve", jwvtok), ("ve", m0_ve[g])] if j == 0
                                  else ()))
                    P.mark("pe")
                    updtok = P.val("pe")
                    last_upd[0] = updtok
                    P.ve(lambda e, g=g, cc=cc: e.tensor_tensor(
                        m0all[:, 6 * g:6 * (g + 1)],
                        bank(6)[:, 128 * g + 32 * (cc % 2):
                                128 * g + 32 * (cc % 2) + 6],
                        m0d[:, 6 * g:6 * (g + 1)], OP.add),
                        wait=[("pe", updtok), ("ve", m0dtok)])
                    P.ve(lambda e, g=g: e.tensor_copy(
                        m0bf[:, 6 * g:6 * (g + 1)], m0all[:, 6 * g:6 * (g + 1)]),
                        inc="ve")
                    m0_ve[g] = P.val("ve")
                P.mark("pe")
                a_slot[par] = P.val("pe")
                mm23tok = P.val("pe")
                P.ve(lambda e, g=g, rb=rb: e.tensor_copy(
                    rdm_sb[:, 512 * g:512 * (g + 1)], bank(rb)),
                    wait=[("pe", mm23tok)], inc="ve")
                rdm_free[par] = P.val("ve")
                rdm_ev[g] = P.val("ve")
            for g in range(4):
                rdmv = bass.AP(rdm_sb.tensor, rdm_sb.offset + 512 * g,
                               [list(rdm_sb.ap[0]), [128, 4], [32, 4], [1, 6]])
                rdv = bass.AP(rd_tok.tensor,
                              rd_tok.offset + 96 * (cb * 4) + 24 * g,
                              [list(rd_tok.ap[0]), [96, 4], [6, 4], [1, 6]])
                p1v = bass.AP(prod1.tensor, prod1.offset + 24 * g,
                              [list(prod1.ap[0]), [96, 4], [6, 4], [1, 6]])
                p2v = bass.AP(prod2.tensor, prod2.offset + 24 * g,
                              [list(prod2.ap[0]), [96, 4], [6, 4], [1, 6]])
                r1 = bass.AP(sigt.tensor, sigt.offset + 4 * g,
                             [list(sigt.ap[0]), [16, 4], [1, 4]])
                r2 = bass.AP(sigt2.tensor, sigt2.offset + 4 * g,
                             [list(sigt2.ap[0]), [16, 4], [1, 4]])
                P.ve(lambda e, rdmv=rdmv, rdv=rdv, p1v=p1v: e.tensor_tensor(
                    p1v, rdmv, rdv, OP.mult))
                P.ve(lambda e, rdmv=rdmv, p2v=p2v: e.tensor_tensor(
                    p2v, rdmv, rdmv, OP.mult))
                P.ve(lambda e, p1v=p1v, r1=r1: e.tensor_reduce(
                    r1, p1v, AX, OP.add))
                P.ve(lambda e, p2v=p2v, r2=r2: e.tensor_reduce(
                    r2, p2v, AX, OP.add))
            msv1 = bass.AP(ms_tok.tensor, ms_tok.offset + 64 * cb + 64 * 0,
                           [list(ms_tok.ap[0]), [16, 4], [4, 4], [1, 4]])
            P.ve(lambda e: e.tensor_tensor(sigt2[:, 0:64], sigt2[:, 0:64],
                                           sigt[:, 0:64], OP.subtract))
            P.ve(lambda e, msv1=msv1, cb=cb: e.scalar_tensor_tensor(
                bass.AP(ms_tok.tensor, ms_tok.offset + 64 * cb,
                        [list(ms_tok.ap[0]), [1, 64]]),
                sigt2[:, 0:64], alcol, sigt[:, 0:64], OP.mult, OP.add),
                inc="ve")

        P.ve(lambda e: e.tensor_tensor(
            sigt.rearrange("p (t h) -> p t h", h=16),
            ms_tok.rearrange("p (t h) -> p t h", h=16),
            bass.AP(mscbc.tensor, mscbc.offset,
                    [list(mscbc.ap[0]), [0, NT], [1, 16]]), OP.mult), inc="ve")
        P.sc(lambda e: e.activation(sigt2[:, 0:256], sigt[:, 0:256], AF.Exp,
                                    bias=0.0, scale=-1.0),
             wait=[("ve", P.val("ve"))], inc="sc")
        P.ve(lambda e: e.tensor_scalar(sigt2[:, 0:256], sigt2[:, 0:256], 1.0,
                                       0.0, OP.add, OP.add),
             wait=[("sc", P.val("sc"))])
        P.ve(lambda e: e.reciprocal(sigt2[:, 0:256], sigt2[:, 0:256]))
        P.ve(lambda e: e.tensor_tensor(sigt[:, 0:256], sigt2[:, 0:256],
                                       gatesig, OP.mult))
        P.ve(lambda e: e.tensor_reduce(
            bass.AP(gcols.tensor, gcols.offset,
                    [list(gcols.ap[0]), [1, NT]]),
            bass.AP(sigt.tensor, sigt.offset,
                    [list(sigt.ap[0]), [16, NT], [1, 16]]), AX, OP.add))
        P.ve(lambda e: e.tensor_single_scalar(gcols, gcols, 1.0 / 16.0,
                                              OP.mult), inc="ve")

    def emit_p4():
        for i in range(2):
            P.ve(lambda e, i=i: e.memset(kaug[i][64:65, :], -1.0))
            P.ve(lambda e, i=i: e.memset(
                vaug[i].rearrange("p (k c) -> p k c", c=66)[:, :, 64:65], 1.0),
                inc="ve")
        obank_free = P.val("ve")
        win_free = {0: P.val("pe"), 1: P.val("pe")}
        exp_g = {0: ("sc", P.val("sc")), 1: ("sc", P.val("sc"))}
        pv_g = {0: P.val("pe"), 1: P.val("pe")}
        otrans_ev = P.val("ve")
        stab_rd = {0: P.val("pe"), 1: P.val("pe")}
        gcnt = 0
        dqv = {}
        for p in range(8):
            buf = p % 2
            toks = []
            for pi, proj in enumerate((0, 1, 2)):
                w = [("pe", win_free[buf])] if pi == 0 else []
                if p == 0 and pi == 0:
                    w.extend(scratch_toks)
                toks.append(P.dma("sy", lambda e, p=p, buf=buf, proj=proj:
                                  e.dma_start(
                    qkvwin[:, 3072 * buf + 1024 * proj:3072 * buf + 1024 * (proj + 1)],
                    qkvT_s[:, 8192 * proj + 1024 * p:8192 * proj + 1024 * (p + 1)]),
                    wait=w))
            dqv[p] = toks
            evac_toks = {}
            for proj in range(3):
                for nb in range(4):
                    for dc in range(8):
                        w = []
                        if proj == 0 and nb == 0 and dc == 0:
                            w = dqv[p] + [("ve", obank_free)]
                        elif nb == 0 and dc == 0:
                            w = list(evac_toks[proj - 1])
                        P.pe(lambda e, proj=proj, nb=nb, dc=dc, buf=buf:
                             e.matmul(
                                 bank(nb),
                                 qkvwin[:, 3072 * buf + 1024 * proj + 128 * dc:
                                        3072 * buf + 1024 * proj + 128 * (dc + 1)],
                                 xb[:, 2048 * dc + 512 * nb:
                                    2048 * dc + 512 * (nb + 1)],
                                 start=(dc == 0), stop=(dc == 7)), wait=w)
                P.mark("pe")
                pev = P.val("pe")
                if proj < 2:
                    dstl = qaug if proj == 0 else kaug
                    bcol = p if proj == 0 else 8 + p
                    sl = bcolq[0:64, bcol:bcol + 1]
                    bc = bass.AP(sl.tensor, sl.offset,
                                 [list(sl.ap[0]), [0, 2048]])
                    P.ve(lambda e, dstl=dstl, bc=bc: e.tensor_tensor(
                        dstl[0][0:64, :], PS[0:64, 0:2048], bc, OP.add),
                        wait=[("pe", pev)], inc="ve")
                    P.sc(lambda e, dstl=dstl, bcol=bcol: e.activation(
                        dstl[1][0:64, :], PS[64:128, 0:2048],
                        AF.Identity, bias=bcolq[64:128, bcol:bcol + 1],
                        scale=1.0), wait=[("pe", pev)], inc="sc")
                    evac_toks[proj] = (("ve", P.val("ve")),
                                       ("sc", P.val("sc")))
                else:
                    sl = bcolq[:, 16 + p:17 + p]
                    bc = bass.AP(sl.tensor, sl.offset,
                                 [list(sl.ap[0]), [0, 2048]])
                    P.ve(lambda e, bc=bc: e.tensor_tensor(
                        vt_st, PS[:, 0:2048], bc, OP.add),
                        wait=[("pe", pev)], inc="ve")
                    evac_toks[proj] = (("ve", P.val("ve")),)
            win_free[buf] = P.val("pe")
            vevac = evac_toks[2]

            for i in range(2):
                bb = 6 + i
                for kc in range(16):
                    idn = ident[0:64, 0:64] if i == 0 else ident64[64:128, :]
                    w = ()
                    if kc == 0:
                        w = list(vevac) + [("ve", P.val("ve"))]
                        if i == 0:
                            w.append(exp_g[1])
                    P.pe(lambda e, i=i, kc=kc, bb=bb, idn=idn: e.transpose(
                        bankb(bb)[:, 64 * kc:64 * (kc + 1)],
                        vt_st[64 * i:64 * (i + 1), 128 * kc:128 * (kc + 1)],
                        idn), wait=w)
                P.mark("pe")
                pev = P.val("pe")
                P.ve(lambda e, i=i, bb=bb: e.tensor_copy(
                    bass.AP(vaug[i].tensor, vaug[i].offset,
                            [list(vaug[i].ap[0]), [66, 16], [1, 64]]),
                    bankb(bb).rearrange("p (k f) -> p k f", f=64)),
                    wait=[("pe", pev)], inc="ve")
            vaug_tok = P.val("ve")

            btoks = {}
            for i in range(2):
                sqb = sq_b if i == 0 else oT_st
                so = 8 * i
                if i == 0:
                    prow = PS[96:97, 0:2048]
                    tp = (0, 96)
                else:
                    prow = PS[64:65, 0:2048]
                    tp = None
                wk = [("pe", stab_rd[i])] + list(evac_toks[1])
                if i == 1:
                    wk.append(("ve", otrans_ev))
                P.ve(lambda e, i=i, sqb=sqb: e.tensor_tensor(
                    sqb[0:64, :], kaug[i][0:64, :], kaug[i][0:64, :],
                    OP.mult), wait=wk, inc="ve")
                sqk = P.val("ve")
                w0 = [("ve", sqk)]
                if i == 0:
                    w0 += list(evac_toks[2])
                else:
                    w0 += [("ve", otrans_ev)]
                for nb in range(4):
                    P.pe(lambda e, nb=nb, sqb=sqb, prow=prow, tp=tp: e.matmul(
                        prow[:, 512 * nb:512 * (nb + 1)],
                        blockones[0:64, 0:1],
                        sqb[0:64, 512 * nb:512 * (nb + 1)],
                        start=True, stop=True, tile_position=tp),
                        wait=w0 if nb == 0 else ())
                P.mark("pe")
                P.ve(lambda e, prow=prow, so=so: e.reduce_max(
                    snb[:, so:so + 1], prow, axis=AX),
                     wait=[("pe", P.val("pe"))], inc="ve")
                mk2tok = P.val("ve")
                k2pe = P.val("pe")
                P.ve(lambda e, i=i, sqb=sqb: e.tensor_tensor(
                    sqb[0:64, :], qaug[i][0:64, :], qaug[i][0:64, :],
                    OP.mult), wait=[("pe", k2pe)] + list(evac_toks[0]),
                    inc="ve")
                sqq = P.val("ve")
                for nb in range(4):
                    P.pe(lambda e, nb=nb, sqb=sqb, prow=prow, tp=tp: e.matmul(
                        prow[:, 512 * nb:512 * (nb + 1)],
                        blockones[0:64, 0:1],
                        sqb[0:64, 512 * nb:512 * (nb + 1)],
                        start=True, stop=True, tile_position=tp),
                        wait=[("ve", sqq), ("ve", mk2tok)] if nb == 0 else ())
                P.mark("pe")
                q2tok = P.val("pe")
                stab_rd[i] = q2tok
                P.ve(lambda e, prow=prow, so=so: e.reduce_max(
                    snb[:, so + 1:so + 2], prow, axis=AX),
                     wait=[("pe", q2tok)], inc="ve")
                P.sc(lambda e, so=so: e.activation(
                    snb[:, so + 2:so + 4], snb[:, so:so + 2], AF.Ln,
                    bias=0.0, scale=1.0),
                     wait=[("ve", P.val("ve"))], inc="sc")
                P.ve(lambda e, so=so: e.tensor_tensor(
                    snb[:, so + 4:so + 5], snb[:, so + 2:so + 3],
                    snb[:, so + 3:so + 4], OP.subtract),
                     wait=[("sc", P.val("sc"))])
                P.ve(lambda e, so=so: e.tensor_tensor(
                    snb[:, so + 5:so + 6], snb[:, so + 2:so + 3],
                    snb[:, so + 3:so + 4], OP.add), inc="ve")
                P.sc(lambda e, so=so: e.activation(
                    snb[:, so + 6:so + 8], snb[:, so + 4:so + 6], AF.Exp,
                    bias=0.0, scale=0.5),
                     wait=[("ve", P.val("ve"))], inc="sc")
                P.ve(lambda e, so=so: e.tensor_scalar(
                    snb[:, so + 6:so + 8], snb[:, so + 6:so + 8], 0.5,
                    0.0, OP.mult, OP.add),
                     wait=[("sc", P.val("sc"))], inc="ve")
                cstok = P.val("ve")
                P.sc(lambda e, i=i, prow=prow, so=so: e.activation(
                    qaug[i][64:65, :], prow, AF.Identity,
                    bias=snb[:, so + 7:so + 8], scale=snb[:, so + 6:so + 7]),
                    wait=[("ve", cstok)], inc="sc")
                btoks[i] = P.val("sc")

            for i in range(2):
                h = 2 * p + i
                groups = []
                for kc in range(16):
                    ps = qpieces(128 * kc, T)
                    for j in range(0, len(ps), 2):
                        groups.append((kc, ps[j:j + 2]))
                first_s = True
                pend_pv = [None]

                def flush_pv():
                    if pend_pv[0] is not None:
                        pend_pv[0]()
                        pend_pv[0] = None
                for kc, pcs in groups:
                    bp = gcnt % 2
                    base = 2048 + 1024 * bp
                    c0s = [a % 1024 for a, _ in pcs]
                    for k, (a, b2) in enumerate(pcs):
                        n = b2 - a
                        w = []
                        if k == 0:
                            w = [exp_g[bp]]
                            if first_s:
                                w.append(("sc", btoks[i]))
                                if i == 0:
                                    w.append(("ve", vaug_tok))
                                first_s = False
                        P.pe(lambda e, i=i, kc=kc, a=a, n=n, base=base,
                             c0=c0s[k]: e.matmul(
                            PS[:, base + c0:base + c0 + n],
                            kaug[i][:, 128 * kc:128 * (kc + 1)],
                            qaug[i][:, a:a + n], start=True, stop=True),
                            wait=w)
                    P.mark("pe")
                    stok = P.val("pe")
                    flush_pv()
                    cmin = min(c0s)
                    cmax = max(c0 + (b2 - a) for c0, (a, b2) in zip(c0s, pcs))
                    P.sc(lambda e, bp=bp, base=base, cmin=cmin, cmax=cmax:
                         e.activation(ptb[bp][:, cmin:cmax],
                                      PS[:, base + cmin:base + cmax],
                                      AF.Exp, bias=0.0, scale=1.0),
                         wait=[("pe", stok), ("pe", pv_g[bp])], inc="sc")
                    exp_g[bp] = ("sc", P.val("sc"))
                    mtok = None
                    if pcs[0][0] == 128 * kc:
                        c0d = (128 * kc) % 1024
                        P.ve(lambda e, bp=bp, c0d=c0d: e.tensor_tensor(
                            ptb[bp][:, c0d:c0d + 128],
                            ptb[bp][:, c0d:c0d + 128],
                            diagmask, OP.mult), wait=[exp_g[bp]], inc="ve")
                        mtok = ("ve", P.val("ve"))

                    def mk_pv(i=i, kc=kc, pcs=pcs, bp=bp, c0s=c0s,
                              eg=exp_g[bp], mtok=mtok):
                        def em():
                            for k, (a, b2) in enumerate(pcs):
                                n = b2 - a
                                bkidx = a // 512
                                last_kc = min(15, 4 * bkidx + 3)
                                w2 = [eg]
                                if mtok is not None and a == 128 * kc:
                                    w2.append(mtok)
                                if kc == 0:
                                    w2.append(("ve", otrans_ev))
                                    if i == 0:
                                        w2.append(("sc", btoks[1]))
                                P.pe(lambda e, i=i, kc=kc, a=a, n=n, bp=bp,
                                     c0=c0s[k], last_kc=last_kc:
                                     e.matmul(PS[0:65, a:a + n],
                                              vaug[i][:, 66 * kc:66 * kc + 65],
                                              ptb[bp][:, c0:c0 + n],
                                              start=(kc == 0),
                                              stop=(kc == last_kc)),
                                     wait=w2)
                            P.mark("pe")
                            pv_g[bp] = P.val("pe")
                        return em
                    pend_pv[0] = mk_pv()
                    gcnt += 1
                flush_pv()
                pvall = P.val("pe")
                P.ve(lambda e: e.tensor_copy(oT_st, PS[0:65, 0:2048]),
                     wait=[("pe", pvall)], inc="ve")
                oevtok = P.val("ve")
                for t in range(NT):
                    w = [("ve", oevtok)] if t == 0 else ()
                    P.pe(lambda e, t=t: e.transpose(
                        bankb(0)[:, 64 * t:64 * (t + 1)],
                        oT_st[0:64, 128 * t:128 * (t + 1)],
                        ident[0:64, 0:64]), wait=w)
                for t in range(NT):
                    P.pe(lambda e, t=t: e.transpose(
                        bankb(1)[:, 2 * t:2 * t + 1],
                        oT_st[64:65, 128 * t:128 * (t + 1)],
                        ident[64:65, 64:65]))
                P.mark("pe")
                ttok = P.val("pe")
                P.ve(lambda e, h=h: e.tensor_copy(
                    bass.AP(seqout.tensor, seqout.offset + 64 * h,
                            [list(seqout.ap[0]), [1024, NT], [1, 64]]),
                    bankb(0).rearrange("p (t f) -> p t f", f=64)),
                    wait=[("pe", ttok)])
                P.ve(lambda e, h=h: e.tensor_copy(
                    bass.AP(ncols.tensor, ncols.offset + h,
                            [list(ncols.ap[0]), [16, NT]]),
                    bass.AP(PSB.tensor, PSB.offset + 1024,
                            [list(PSB.ap[0]), [2, NT]])), inc="ve")
                otrans_ev = P.val("ve")
            obank_free = otrans_ev

    def emit_p5p6():
        dwv = []
        dwv.append(P.dma("gp", lambda e: e.dma_start(outb_row, b_out.unsqueeze(0))))
        dwv.append(P.dma("gp", lambda e: e.dma_start(mvb_row, b_mv.unsqueeze(0))))
        dwv.append(P.dma("sy", lambda e: e.dma_start(mvout_w[:, 0:8192], mvT_s),
                         wait=scratch_toks))
        dwv.append(P.dma("sy", lambda e: e.dma_start(mvout_w[:, 8192:16384],
                                                     outT_s)))
        P.ve(lambda e: e.memset(statS, 0.0))
        P.ve(lambda e: e.memset(statQ, 0.0))
        for t in range(NT):
            P.ve(lambda e, t=t: e.reciprocal(p5col, ncols[:, 16 * t:16 * (t + 1)]))
            P.ve(lambda e, t=t: e.tensor_tensor(
                seqout[:, 1024 * t:1024 * (t + 1)],
                seqout[:, 1024 * t:1024 * (t + 1)],
                bass.AP(p5col.tensor, p5col.offset,
                        [list(p5col.ap[0]), [1, 16], [0, 64]]), OP.mult),
                inc=("ve" if t == NT - 1 else None))
        nrmtok = P.val("ve")
        ev_guard = {0: P.val("ve"), 1: P.val("ve")}
        for t in range(NT):
            par = t % 2
            b0 = par * 2
            for dc in range(8):
                for half in range(2):
                    w = []
                    if dc == 0 and half == 0:
                        w = dwv + [("ve", max(nrmtok, ev_guard[par]))]
                    P.pe(lambda e, t=t, dc=dc, half=half, b0=b0: e.matmul(
                        bank(b0 + half),
                        xb[:, 2048 * dc + 128 * t:2048 * dc + 128 * (t + 1)],
                        mvout_w[:, 1024 * dc + 512 * half:1024 * dc + 512 * (half + 1)],
                        start=(dc == 0), stop=False), wait=w)
            for half in range(2):
                P.pe(lambda e, half=half, b0=b0: e.matmul(
                    bank(b0 + half), onesrow,
                    mvb_row[:, 512 * half:512 * (half + 1)],
                    start=False, stop=True))
            P.mark("pe")
            pev = P.val("pe")
            P.ve(lambda e, t=t, b0=b0: e.scalar_tensor_tensor(
                seqout[:, 1024 * t:1024 * (t + 1)], bank(b0, 2),
                gcols[:, t:t + 1], seqout[:, 1024 * t:1024 * (t + 1)],
                OP.mult, OP.add), wait=[("pe", pev)], inc="ve")
            ev_guard[par] = P.val("ve")
        cmb_guard = P.val("pe")
        cmb_ev = P.val("ve")
        ev_guard = {0: P.val("ve"), 1: P.val("ve")}
        for t in range(NT):
            par = t % 2
            b0 = par * 2
            for dc in range(8):
                w = [("ve", cmb_ev)] if dc == 0 else ()
                P.pe(lambda e, t=t, dc=dc: e.transpose(
                    bankb(4)[:, 128 * dc:128 * (dc + 1)],
                    seqout[:, 1024 * t + 128 * dc:1024 * t + 128 * (dc + 1)],
                    ident), wait=w)
            P.mark("pe")
            pev = P.val("pe")
            P.ve(lambda e: e.tensor_copy(cmbT, bankb(4)),
                 wait=[("pe", pev), ("pe", cmb_guard)], inc="ve")
            cmbtok = P.val("ve")
            cmb_ev = cmbtok
            for dc in range(8):
                for half in range(2):
                    w = [("ve", cmbtok)] if (dc == 0 and half == 0) else ()
                    P.pe(lambda e, dc=dc, half=half, b0=b0: e.matmul(
                        bank(b0 + half), cmbT[:, 128 * dc:128 * (dc + 1)],
                        mvout_w[:, 8192 + 1024 * dc + 512 * half:
                                8192 + 1024 * dc + 512 * (half + 1)],
                        start=(dc == 0), stop=False), wait=w)
            for half in range(2):
                P.pe(lambda e, half=half, b0=b0: e.matmul(
                    bank(b0 + half), onesrow,
                    outb_row[:, 512 * half:512 * (half + 1)],
                    start=False, stop=True))
            P.mark("pe")
            cmb_guard = P.val("pe")
            pev = P.val("pe")
            P.ve(lambda e, t=t, b0=b0: e.scalar_tensor_tensor(
                xa[:, 1024 * t:1024 * (t + 1)], bank(b0, 2), 1.0,
                xa[:, 1024 * t:1024 * (t + 1)], OP.mult, OP.add,
                accum_out=statS[:, t:t + 1]), wait=[("pe", pev)], inc="ve")
            ev_guard[par] = P.val("ve")
            P.sc(lambda e, t=t: e.activation(
                ln_scrap, xa[:, 1024 * t:1024 * (t + 1)], AF.Square,
                bias=0.0, scale=1.0, accum_out=statQ[:, t:t + 1]),
                wait=[("ve", ev_guard[par])], inc="sc")
        stattok = P.val("sc")
        P.ve(lambda e: e.tensor_scalar(statS, statS, 1.0 / D, 0.0, OP.mult,
                                       OP.add), wait=[("sc", stattok)])
        P.ve(lambda e: e.tensor_scalar(statQ, statQ, 1.0 / D, 0.0, OP.mult,
                                       OP.add))
        P.ve(lambda e: e.tensor_tensor(ln_scrap[:, 0:NT], statS, statS,
                                       OP.mult))
        P.ve(lambda e: e.tensor_tensor(statQ, statQ, ln_scrap[:, 0:NT],
                                       OP.subtract))
        P.ve(lambda e: e.tensor_single_scalar(statQ, statQ, LN_EPS, OP.add),
             inc="ve")
        P.sc(lambda e: e.activation(statR, statQ, AF.Ln, bias=0.0, scale=1.0),
             wait=[("ve", P.val("ve"))], inc="sc")
        P.sc(lambda e: e.activation(statR, statR, AF.Exp, bias=0.0, scale=-0.5),
             inc="sc")
        P.ve(lambda e: e.tensor_tensor(statM, statS, statR, OP.mult),
             wait=[("sc", P.val("sc"))])
        P.ve(lambda e: e.tensor_single_scalar(statM, statM, -1.0, OP.mult),
             inc="ve")
        lntok = P.val("ve")
        for t in range(NT):
            P.sc(lambda e, t=t: e.activation(
                xa[:, 1024 * t:1024 * (t + 1)], xa[:, 1024 * t:1024 * (t + 1)],
                AF.Identity, bias=statM[:, t:t + 1], scale=statR[:, t:t + 1]),
                wait=([("ve", lntok)] if t == 0 else ()),
                inc=("sc" if t == NT - 1 else None))

    def emit_p7(b):
        dwv = []
        dwv.append(P.dma("sy", lambda e: e.dma_start(wguT, wguT_s),
                         wait=scratch_toks))
        dwv.append(P.dma("sy", lambda e: e.dma_start(wdTr, wdT_s)))
        P.ve(lambda e: e.memset(f2S, 0.0))
        P.ve(lambda e: e.memset(f2Q, 0.0))
        h1tok = P.val("sc")
        xqf = PS[:, 3072:4096]
        xqT_b = [xqT, xqT1]
        dsl_b = [dsl0, xq_t]
        hqb_b = [hqb0, hqb0]
        fc_b = [fcol, fcol2]
        xqT_toks = {}
        xqT_free = {0: P.val("pe"), 1: P.val("pe")}
        st = {"b6": P.val("ve"), "b7": P.val("ve"),
              "xqt_rd": P.val("pe"),
              "dsl": {0: P.val("pe"), 1: P.val("pe")},
              "hqb": {0: P.val("pe"), 1: P.val("pe")}}

        def emit_xq(t):
            fc = fc_b[t % 2]
            xat = xa[:, 1024 * t:1024 * (t + 1)]
            P.ve(lambda e, xat=xat, fc=fc: e.reduce_max(
                fc[:, 0:1], xat, axis=AX, apply_absolute_value=True),
                wait=[("sc", h1tok)] if t == 0 else ())
            P.ve(lambda e, fc=fc: e.tensor_single_scalar(
                fc[:, 1:2], fc[:, 0:1], 1e-5, OP.max))
            P.ve(lambda e, fc=fc: e.reciprocal(fc[:, 2:3], fc[:, 1:2]))
            P.ve(lambda e, fc=fc: e.tensor_single_scalar(
                fc[:, 2:3], fc[:, 2:3], 127.0, OP.mult))
            P.ve(lambda e, fc=fc: e.tensor_single_scalar(
                fc[:, 3:4], fc[:, 1:2], 1.0 / 127.0, OP.mult))
            P.ve(lambda e, fc=fc: e.tensor_tensor(
                fc[:, 4:5], fc[:, 3:4], mwcol[:, 0:1], OP.mult))
            P.ve(lambda e, fc=fc: e.tensor_tensor(
                fc[:, 5:6], fc[:, 3:4], mwcol[:, 1:2], OP.mult), inc="ve")
            colstok = P.val("ve")
            P.sc(lambda e, xat=xat, fc=fc: e.activation(
                xqf, xat, AF.Identity, bias=magiccol, scale=fc[:, 2:3]),
                wait=[("ve", colstok), ("ve", st["b6"]), ("ve", st["b7"])],
                inc="sc")
            P.ve(lambda e: e.tensor_single_scalar(
                xq_t, xqf, MAGIC, OP.subtract),
                wait=[("sc", P.val("sc")), ("pe", st["dsl"][1])], inc="ve")
            xqtok = P.val("ve")
            for dc in range(8):
                w = [("ve", xqtok)] if dc == 0 else ()
                P.pe(lambda e, dc=dc: e.transpose(
                    bankb(6)[:, 128 * dc:128 * (dc + 1)],
                    xq_t[:, 128 * dc:128 * (dc + 1)], ident), wait=w)
            P.mark("pe")
            st["xqt_rd"] = P.val("pe")
            P.ve(lambda e, t=t: e.tensor_copy(xqT_b[t % 2], bankb(6)),
                 wait=[("pe", st["xqt_rd"]), ("pe", xqT_free[t % 2])],
                 inc="ve")
            st["b6"] = P.val("ve")
            xqT_toks[t] = P.val("ve")

        emit_xq(0)
        prev_res = P.val("ve")
        hread_g = {0: P.val("ve"), 1: P.val("ve")}
        silu_g = {0: P.val("sc"), 1: P.val("sc")}
        silu_b = [silu_t, silu_t2]
        stt_b = {0: P.val("ve"), 1: P.val("ve")}
        gu_pre = {}

        def emit_silu(t, og, gutok):
            fc = fc_b[t % 2]
            gb = (og % 2) * 2
            P.sc(lambda e, gb=gb, fc=fc, og=og: e.activation(
                silu_b[og % 2], bank(gb), AF.Silu, bias=0.0,
                scale=fc[:, 4:5]),
                wait=[("pe", gutok), ("ve", stt_b[og % 2])], inc="sc")
            silu_g[og % 2] = P.val("sc")
            return P.val("sc")

        def emit_gu(t, og, extra=()):
            gb = (og % 2) * 2
            xT = xqT_b[t % 2]
            for dc in range(8):
                w = []
                if dc == 0:
                    w = [("ve", xqT_toks[t]), ("sc", silu_g[og % 2]),
                         ("ve", hread_g[og % 2])] + list(extra)
                P.pe(lambda e, og=og, dc=dc, gb=gb, xT=xT: e.matmul(
                    bank(gb), xT[:, 128 * dc:128 * (dc + 1)],
                    wguT[:, 8192 * dc + 512 * og:8192 * dc + 512 * (og + 1)],
                    start=(dc == 0), stop=(dc == 7)), wait=w)
            for dc in range(8):
                P.pe(lambda e, og=og, dc=dc, gb=gb, xT=xT: e.matmul(
                    bank(gb + 1), xT[:, 128 * dc:128 * (dc + 1)],
                    wguT[:, 8192 * dc + 4096 + 512 * og:
                         8192 * dc + 4096 + 512 * (og + 1)],
                    start=(dc == 0), stop=(dc == 7)))
            P.mark("pe")
            return P.val("pe")

        for t in range(NT):
            xat = xa[:, 1024 * t:1024 * (t + 1)]
            fc = fc_b[t % 2]
            for og in range(8):
                gb = (og % 2) * 2
                if (t, og) in gu_pre:
                    gutok, siltok = gu_pre.pop((t, og))
                else:
                    gutok = emit_gu(t, og,
                                    extra=(dwv if t == 0 and og == 0 else ()))
                    siltok = None
                if og == 7:
                    xqT_free[t % 2] = gutok
                if siltok is None:
                    siltok = emit_silu(t, og, gutok)
                P.ve(lambda e, og=og, gb=gb, fc=fc: e.scalar_tensor_tensor(
                    htile[:, 512 * og:512 * (og + 1)], bank(gb + 1),
                    fc[:, 5:6], silu_b[og % 2], OP.mult, OP.mult),
                    wait=[("sc", siltok)], inc="ve")
                stt_b[og % 2] = P.val("ve")
                P.ve(lambda e, og=og: e.reduce_max(
                    hpart[:, og:og + 1], htile[:, 512 * og:512 * (og + 1)],
                    axis=AX, apply_absolute_value=True), inc="ve")
                hread_g[og % 2] = P.val("ve")
                if og == 2 and t + 1 < NT:
                    emit_xq(t + 1)
            if t + 1 < NT:
                g0 = emit_gu(t + 1, 0)
                s0 = emit_silu(t + 1, 0, g0)
                g1 = emit_gu(t + 1, 1)
                s1 = emit_silu(t + 1, 1, g1)
                gu_pre[(t + 1, 0)] = (g0, s0)
                gu_pre[(t + 1, 1)] = (g1, s1)
            P.ve(lambda e, fc=fc: e.reduce_max(fc[:, 6:7], hpart, axis=AX),
                 wait=[("ve", hread_g[1])])
            P.ve(lambda e, fc=fc: e.tensor_single_scalar(
                fc[:, 6:7], fc[:, 6:7], 1e-5, OP.max))
            P.ve(lambda e, fc=fc: e.reciprocal(fc[:, 7:8], fc[:, 6:7]))
            P.ve(lambda e, fc=fc: e.tensor_single_scalar(
                fc[:, 7:8], fc[:, 7:8], 127.0, OP.mult))
            P.ve(lambda e, fc=fc: e.tensor_single_scalar(
                fc[:, 0:1], fc[:, 6:7], 1.0 / 127.0, OP.mult))
            P.ve(lambda e, fc=fc: e.tensor_tensor(
                fc[:, 0:1], fc[:, 0:1], mwcol[:, 2:3], OP.mult), inc="ve")
            hsctok = P.val("ve")
            for icq in range(4):
                sl = icq % 2
                P.sc(lambda e, icq=icq, fc=fc: e.activation(
                    htile[:, 1024 * icq:1024 * (icq + 1)],
                    htile[:, 1024 * icq:1024 * (icq + 1)],
                    AF.Identity, bias=magiccol, scale=fc[:, 7:8]),
                    wait=[("ve", hsctok)] if icq == 0 else (), inc="sc")
                P.ve(lambda e, icq=icq: e.tensor_single_scalar(
                    hqb_b[0],
                    htile[:, 1024 * icq:1024 * (icq + 1)],
                    MAGIC, OP.subtract),
                    wait=[("sc", P.val("sc")), ("pe", st["hqb"][0])],
                    inc="ve")
                hqtok = P.val("ve")
                for ii in range(8):
                    w = []
                    if ii == 0:
                        w = [("ve", hqtok), ("ve", st["b7"])]
                    P.pe(lambda e, ii=ii: e.transpose(
                        bankb(7)[:, 128 * ii:128 * (ii + 1)],
                        hqb_b[0][:, 128 * ii:128 * (ii + 1)],
                        ident), wait=w)
                P.mark("pe")
                st["hqb"][0] = P.val("pe")
                wv = [("pe", st["hqb"][0]), ("pe", st["dsl"][sl])]
                if sl == 1:
                    wv.append(("pe", st["xqt_rd"]))
                P.ve(lambda e, sl=sl: e.tensor_copy(dsl_b[sl], bankb(7)),
                     wait=wv, inc="ve")
                st["b7"] = P.val("ve")
                slot_ev = P.val("ve")
                for ii in range(8):
                    ic = icq * 8 + ii
                    for half in range(2):
                        w = []
                        if ii == 0 and half == 0:
                            w = [("ve", slot_ev)]
                        P.pe(lambda e, ic=ic, ii=ii, half=half, sl=sl:
                             e.matmul(
                                 bank(4 + half),
                                 dsl_b[sl][:, 128 * ii:128 * (ii + 1)],
                                 wdTr[:, 1024 * ic + 512 * half:
                                      1024 * ic + 512 * (half + 1)],
                                 start=(ic == 0), stop=(ic == 31)), wait=w)
                P.mark("pe")
                st["dsl"][sl] = P.val("pe")
            downtok = P.val("pe")
            P.ve(lambda e, t=t, xat=xat, fc=fc: e.scalar_tensor_tensor(
                xat, bank(4, 2), fc[:, 0:1], xat, OP.mult, OP.add,
                accum_out=f2S[:, t:t + 1]),
                wait=[("pe", downtok), ("ve", prev_res)], inc="ve")
            prev_res = P.val("ve")
            P.sc(lambda e, t=t, xat=xat: e.activation(
                sq_scr, xat, AF.Square, bias=0.0, scale=1.0,
                accum_out=f2Q[:, t:t + 1]),
                wait=[("ve", prev_res)], inc="sc")
        stattok = P.val("sc")
        P.ve(lambda e: e.tensor_scalar(f2S, f2S, 1.0 / D, 0.0, OP.mult, OP.add),
             wait=[("sc", stattok)])
        P.ve(lambda e: e.tensor_scalar(f2Q, f2Q, 1.0 / D, 0.0, OP.mult, OP.add))
        P.ve(lambda e: e.tensor_tensor(f2M, f2S, f2S, OP.mult))
        P.ve(lambda e: e.tensor_tensor(f2Q, f2Q, f2M, OP.subtract))
        P.ve(lambda e: e.tensor_single_scalar(f2Q, f2Q, LN_EPS, OP.add),
             inc="ve")
        P.sc(lambda e: e.activation(f2R, f2Q, AF.Ln, bias=0.0, scale=1.0),
             wait=[("ve", P.val("ve"))], inc="sc")
        P.sc(lambda e: e.activation(f2R, f2R, AF.Exp, bias=0.0, scale=-0.5),
             inc="sc")
        P.ve(lambda e: e.tensor_tensor(f2M, f2S, f2R, OP.mult),
             wait=[("sc", P.val("sc"))])
        P.ve(lambda e: e.tensor_single_scalar(f2M, f2M, -1.0, OP.mult),
             inc="ve")
        ln2tok = P.val("ve")
        for t in range(NT):
            P.sc(lambda e, t=t: e.activation(
                xa[:, 1024 * t:1024 * (t + 1)], xa[:, 1024 * t:1024 * (t + 1)],
                AF.Identity, bias=f2M[:, t:t + 1], scale=f2R[:, t:t + 1]),
                wait=([("ve", ln2tok)] if t == 0 else ()), inc="sc")
            sctok = P.val("sc")
            out_toks[b].append(P.dma("sy", lambda e, t=t, b=b: e.dma_start(
                y_d[b].rearrange("(t q) d -> q t d", q=128)[:, t, :],
                xa[:, 1024 * t:1024 * (t + 1)]),
                wait=[("sc", sctok)]))

    def stop(tag):
        dbg_dump(tag)
        if stop_after == tag:
            P.sy(lambda e: None, wait=P.dma_all())
            P.barrier()
            P.build_block()
            return True
        return False

    emit_prep_ffn_dma("g")
    emit_consts()
    if stop("consts"):
        return P
    emit_prep_small()
    emit_prep_big()
    emit_prep_ffn()
    P.sy(lambda e: None, wait=P.dma_all())
    P.barrier()
    if stop("prep"):
        return P

    for b in range(B_LOC):
        emit_p0(b)
        P.barrier()
        if stop(f"p0_{b}"):
            return P
        emit_p2()
        P.barrier()
        if stop(f"p2_{b}"):
            return P
        emit_p3()
        P.barrier()
        if stop(f"p3_{b}"):
            return P
        emit_p4()
        P.barrier()
        if stop(f"p4_{b}"):
            return P
        emit_p5p6()
        P.barrier()
        if stop(f"p56_{b}"):
            return P
        emit_p7(b)
        P.barrier()
        if stop(f"p7_{b}"):
            return P

    P.sy(lambda e: None, wait=P.dma_all())
    P.barrier()
    P.build_block()
    return P


_NC_CACHE = {}


def _build():
    if "nc" not in _NC_CACHE:
        nc = bass.Bass("TRN2", target_bir_lowering=False)
        build_core(nc)
        _NC_CACHE["nc"] = nc
    return _NC_CACHE["nc"]


def kernel(**inputs):
    x = np.ascontiguousarray(inputs["L0_attn_x"], dtype=np.float32)
    common = {
        "w_qkv": np.ascontiguousarray(inputs["L0_attn_qkv_w"], np.float32),
        "b_qkv": np.ascontiguousarray(inputs["L0_attn_qkv_b"], np.float32),
        "w_w1": np.ascontiguousarray(inputs["L0_attn_w1w"], np.float32),
        "w_w2": np.ascontiguousarray(inputs["L0_attn_w2w"], np.float32),
        "w_r1": np.ascontiguousarray(inputs["L0_attn_r1w"], np.float32),
        "w_r2": np.ascontiguousarray(inputs["L0_attn_r2w"], np.float32),
        "w_mv": np.ascontiguousarray(inputs["L0_attn_memv_w"], np.float32),
        "b_mv": np.ascontiguousarray(inputs["L0_attn_memv_b"], np.float32),
        "w_mg": np.ascontiguousarray(inputs["L0_attn_memg_w"], np.float32),
        "b_mg": np.ascontiguousarray(inputs["L0_attn_memg_b"], np.float32),
        "mem_scale": np.ascontiguousarray(inputs["L0_attn_mem_scale"], np.float32),
        "iter_mix": np.ascontiguousarray(
            np.asarray(inputs["L0_attn_iter_mix"], np.float32).reshape(1)),
        "w_out": np.ascontiguousarray(inputs["L0_attn_out_w"], np.float32),
        "b_out": np.ascontiguousarray(inputs["L0_attn_out_b"], np.float32),
        "dlog": np.ascontiguousarray(inputs["L0_attn_decay_logits"], np.float32),
        "w_g": np.ascontiguousarray(inputs["L0_ffn_w_gate"], np.float32),
        "w_u": np.ascontiguousarray(inputs["L0_ffn_w_up"], np.float32),
        "w_dn": np.ascontiguousarray(inputs["L0_ffn_w_down"], np.float32),
    }
    nc = _build()
    in_maps = []
    for c in range(NCORES):
        m = dict(common)
        m["x"] = np.ascontiguousarray(x[c * B_LOC:(c + 1) * B_LOC])
        in_maps.append(m)
    res = run_bass_kernel_spmd(nc, in_maps, core_ids=list(range(NCORES)))
    y = np.concatenate([r["y"] for r in res.results], axis=0)
    return y.reshape(B_GLOB, T, D)


# version 12 (final)
# speedup vs baseline: 1.0269x; 1.0269x over previous
import numpy as np
import concourse.bass as bass
import concourse.mybir as mybir
from concourse.bass_utils import run_bass_kernel_spmd

F32 = mybir.dt.float32
BF16 = mybir.dt.bfloat16
FP8 = mybir.dt.float8e4
OP = mybir.AluOpType
AF = mybir.ActivationFunctionType
AX = mybir.AxisListType.X

MAGIC = 3.0 * 2.0 ** 22
B_GLOB, T, D, H, DH, PD = 16, 2048, 1024, 16, 64, 6
NCORES = 8
B_LOC = B_GLOB // NCORES
NT = T // 128
INNER = 4096
QKV_SCALE = float(DH) ** -0.5
LN_EPS = 1e-5

ENGS = ("pe", "ve", "sc", "gp", "sy")


class Prog:
    def __init__(self, nc):
        self.nc = nc
        self.ops = {e: [] for e in ENGS}
        self.cnt = {}
        self._guards = []
        self.sems = {}
        self.dbg_outs = []
        self.dma_idx = 0
        self.dmg_idx = 0

    def sem(self, name):
        if name not in self.sems:
            g = self.nc.semaphore(name)
            self.sems[name] = g.__enter__()
            self._guards.append(g)
            self.cnt[name] = 0
        return name

    def val(self, name):
        self.sem(name)
        return self.cnt[name]

    CHAINED = {"ve": "ve", "sc": "sc", "gp": "gp"}

    def _emit(self, eng, fn, wait, inc, incn):
        for s, _ in wait:
            self.sem(s)
        wait = list(wait)
        chain = self.CHAINED.get(eng)
        if chain is not None and fn is not None:
            self.sem(chain)
            wait.append((chain, self.cnt[chain]))
            if inc is None:
                inc, incn = chain, 1
        if inc is not None:
            self.sem(inc)
            self.cnt[inc] += incn
        self.ops[eng].append([fn, tuple(wait), inc, incn])

    def mark(self, eng, inc="pe", incn=1):
        spec = self.ops[eng][-1]
        if spec[2] == inc:
            return
        assert spec[2] is None, (eng, spec[2])
        self.sem(inc)
        self.cnt[inc] += incn
        spec[2] = inc
        spec[3] = incn

    @staticmethod
    def _mkop(spec, sems):
        fn, w, inc, incn = spec

        def op(e):
            for s, v in w:
                if v > 0:
                    e.wait_ge(sems[s], v)
            ins = fn(e) if fn is not None else None
            if inc is not None:
                if ins is None:
                    ins = e.nop()
                ins.then_inc(sems[inc], incn)
        return op

    DMA_RING = 6

    def dma(self, eng, fn, wait=()):
        if eng == "gp":
            s = self.sem(f"dmg{self.dmg_idx % 2}")
            self.dmg_idx += 1
        else:
            s = self.sem(f"dma{self.dma_idx % self.DMA_RING}")
            self.dma_idx += 1
        w = list(wait) + [(s, self.cnt[s])]
        self._emit(eng, fn, w, s, 16)
        return (s, self.cnt[s])

    def dma_big(self, k, fn, wait=()):
        s = self.sem(f"dmw{k % 8}")
        w = list(wait) + [(s, self.cnt[s])]
        self._emit("sy", fn, w, s, 16)
        return (s, self.cnt[s])

    def dma_all(self):
        out = [(f"dma{i}", self.val(f"dma{i}")) for i in range(self.DMA_RING)]
        out += [(f"dmg{i}", self.val(f"dmg{i}")) for i in range(2)]
        out += [(f"dmw{i}", self.val(f"dmw{i}")) for i in range(8)]
        return [t for t in out if t[1] > 0]

    def pe(self, fn, wait=(), inc=None, incn=1):
        self._emit("pe", fn, wait, inc, incn)

    def ve(self, fn, wait=(), inc=None, incn=1):
        self._emit("ve", fn, wait, inc, incn)

    def sc(self, fn, wait=(), inc=None, incn=1):
        self._emit("sc", fn, wait, inc, incn)

    def gp(self, fn, wait=(), inc=None, incn=1):
        self._emit("gp", fn, wait, inc, incn)

    def sy(self, fn, wait=(), inc=None, incn=1):
        self._emit("sy", fn, wait, inc, incn)

    def barrier(self):
        own = {"ve": ("ve", self.val("ve")), "sc": ("sc", self.val("sc")),
               "gp": ("gp", self.val("gp")), "pe": ("pe", self.val("pe")),
               "sy": None}
        vals = {}
        for e in ENGS:
            s = self.sem("bar_" + e)
            self.cnt[s] += 1
            vals[e] = self.cnt[s]
        for e in ENGS:
            pre = own[e]
            waits = [("bar_" + o, vals[o]) for o in ENGS if o != e]

            def op(eng, waits=waits, s="bar_" + e, pre=pre):
                if pre is not None and pre[1] > 0:
                    eng.wait_ge(self.sems[pre[0]], pre[1])
                eng.nop().then_inc(self.sems[s], 1)
                for so, v in waits:
                    eng.wait_ge(self.sems[so], v)
            self.ops[e].append([None, (), None, 1, op])

    def build_block(self):
        nc = self.nc

        def runner(eng_ops, e):
            for spec in eng_ops:
                if len(spec) == 5:
                    spec[4](e)
                else:
                    self._mkop(spec, self.sems)(e)

        with nc.Block() as block:
            @block.tensor
            def _(t):
                runner(self.ops["pe"], t)

            @block.vector
            def _(v):
                runner(self.ops["ve"], v)

            @block.scalar
            def _(s):
                runner(self.ops["sc"], s)

            @block.gpsimd
            def _(g):
                runner(self.ops["gp"], g)

            @block.sync
            def _(s):
                runner(self.ops["sy"], s)


def qpieces(start, end, edge=512):
    out, cur = [], start
    while cur < end:
        nxt = min(end, (cur // edge + 1) * edge)
        out.append((cur, nxt))
        cur = nxt
    return out


def build_core(nc, stop_after=None, taps=()):
    P = Prog(nc)
    tens = {}

    ARENA_SZ = 211008
    BASE = (nc.sbuf_base + 63) // 64 * 64
    nc.alloc_sbuf_tensor("arena_reserve",
                         [128, BASE - nc.sbuf_base + ARENA_SZ], mybir.dt.uint8)

    def at(name, off, shape, dtype):
        ap = nc.alloc_sbuf_tensor_at(name, shape, dtype, offset=BASE + off).ap()
        tens[name] = ap
        return ap

    x_d = nc.dram_tensor("x", [B_LOC, T, D], F32, kind="ExternalInput").ap()
    y_d = nc.dram_tensor("y", [B_LOC, T, D], F32, kind="ExternalOutput").ap()
    w_qkv = nc.dram_tensor("w_qkv", [3 * D, D], F32, kind="ExternalInput").ap()
    b_qkv = nc.dram_tensor("b_qkv", [3 * D], F32, kind="ExternalInput").ap()
    w_wl = {n: nc.dram_tensor("w_" + n, [64, D], F32, kind="ExternalInput").ap()
            for n in ("w1", "w2", "r1", "r2")}
    w_mv = nc.dram_tensor("w_mv", [D, D], F32, kind="ExternalInput").ap()
    b_mv = nc.dram_tensor("b_mv", [D], F32, kind="ExternalInput").ap()
    w_mg = nc.dram_tensor("w_mg", [H, D], F32, kind="ExternalInput").ap()
    b_mg = nc.dram_tensor("b_mg", [H], F32, kind="ExternalInput").ap()
    mem_scale_d = nc.dram_tensor("mem_scale", [H], F32, kind="ExternalInput").ap()
    iter_mix_d = nc.dram_tensor("iter_mix", [1], F32, kind="ExternalInput").ap()
    w_out = nc.dram_tensor("w_out", [D, D], F32, kind="ExternalInput").ap()
    b_out = nc.dram_tensor("b_out", [D], F32, kind="ExternalInput").ap()
    dlog_d = nc.dram_tensor("dlog", [H], F32, kind="ExternalInput").ap()
    w_g = nc.dram_tensor("w_g", [INNER, D], F32, kind="ExternalInput").ap()
    w_u = nc.dram_tensor("w_u", [INNER, D], F32, kind="ExternalInput").ap()
    w_dn = nc.dram_tensor("w_dn", [D, INNER], F32, kind="ExternalInput").ap()

    qkvT_s = nc.dram_tensor("qkvT_s", [128, 3 * 8192], BF16).ap()
    mvT_s = nc.dram_tensor("mvT_s", [128, 8192], BF16).ap()
    outT_s = nc.dram_tensor("outT_s", [128, 8192], BF16).ap()
    wguT_s = nc.dram_tensor("wguT_s", [128, 8 * 8192], FP8).ap()
    wdT_s = nc.dram_tensor("wdT_s", [128, 32 * 1024], FP8).ap()
    tens.update(qkvT_s=qkvT_s, mvT_s=mvT_s, outT_s=outT_s, wguT_s=wguT_s,
                wdT_s=wdT_s)

    PS = nc.alloc_psum_tensor("PS", [128, 4096], F32).ap()
    PSB = PS.bitcast(BF16)

    def bank(i, n=1):
        return PS[:, 512 * i:512 * (i + n)]

    def bankb(i, n=1):
        return PSB[:, 1024 * i:1024 * (i + n)]

    _cur = [0]

    def cat(name, shape, dtype, sz):
        ap = at(name, _cur[0], shape, dtype)
        _cur[0] += (sz + 63) // 64 * 64
        return ap

    ident = cat("ident", [128, 128], BF16, 256)
    identf = cat("identf", [128, 128], F32, 512)
    ident64 = cat("ident64", [128, 64], BF16, 128)
    diagmask = cat("diagmask", [128, 128], BF16, 256)
    strictm = cat("strictm", [128, 128], F32, 512)
    dmats = cat("dmats", [128, 16 * 128], BF16, 4096)
    onesrow = cat("onesrow", [1, 128], BF16, 256)
    onesrow_f = cat("onesrow_f", [1, 128], F32, 512)
    onescol_f = cat("onescol_f", [128, 1], F32, 4)
    blockones = cat("blockones", [128, 2], BF16, 4)
    vcols = cat("vcols", [128, 16], F32, 64)
    scaleg = cat("scaleg", [128, 4 * 128], BF16, 1024)
    dcolC = cat("dcolC", [128, 4], F32, 16)
    alcol = cat("alcol", [128, 1], F32, 4)
    mscbc = cat("mscbc", [128, 16], F32, 64)
    bcolq = cat("bcolq", [128, 24], F32, 96)
    mgbcol = cat("mgbcol", [16, 1], F32, 4)
    wlT = cat("wlT", [128, 8 * 272], BF16, 4352)
    mwcol = cat("mwcol", [128, 4], F32, 16)
    c127 = cat("c127", [128, 1], F32, 4)
    prow = cat("prow", [1, 128], F32, 512)
    pcol = cat("pcol", [128, 1], F32, 4)
    c128 = cat("c128", [128, 1], F32, 4)
    rowa = cat("rowa", [1, 16], F32, 64)
    rowb = cat("rowb", [1, 16], F32, 64)
    rowc = cat("rowc", [1, 16], F32, 64)
    rown = cat("rown", [1, 16], F32, 64)
    arow_b = cat("arow_b", [1, 128], F32, 512)
    brow_b = cat("brow_b", [1, 128], F32, 512)
    s4 = cat("s4", [1, 8], F32, 32)
    lndbc = cat("lndbc", [128, 16], F32, 64)
    snb = cat("snb", [1, 16], F32, 64)
    mgrow = cat("mgrow", [1, 16], F32, 64)
    mgrow_bf = cat("mgrow_bf", [1, 16], BF16, 32)
    fcol = cat("fcol", [128, 8], F32, 32)
    fcol2 = cat("fcol2", [128, 8], F32, 32)
    hpart = cat("hpart", [128, 8], F32, 32)
    f2S = cat("f2S", [128, NT], F32, 64)
    f2Q = cat("f2Q", [128, NT], F32, 64)
    f2R = cat("f2R", [128, NT], F32, 64)
    f2M = cat("f2M", [128, NT], F32, 64)
    statS = cat("statS", [128, NT], F32, 64)
    statQ = cat("statQ", [128, NT], F32, 64)
    statR = cat("statR", [128, NT], F32, 64)
    statM = cat("statM", [128, NT], F32, 64)
    p5col = cat("p5col", [128, 16], F32, 64)
    magiccol = cat("magiccol", [128, 1], F32, 4)
    assert _cur[0] <= 16384, _cur[0]

    CB = 16384
    xa = at("xa", CB, [128, NT * 1024], F32)
    xb = at("xb", CB + 65536, [128, 8 * T], BF16)
    ms_tok = at("ms_tok", CB + 98304, [128, NT * 16], F32)
    gatesig = at("gatesig", CB + 99328, [128, NT * 16], F32)
    gcols = at("gcols", CB + 100352, [128, NT], F32)
    ncols = at("ncols", CB + 100416, [128, NT * 16], F32)
    MEMB = CB + 101440
    TOP = 211008

    mo = MEMB
    w12_tok = at("w12_tok", mo, [128, NT * 128], BF16); mo += 4096
    r12_tok = at("r12_tok", mo, [128, NT * 128], BF16); mo += 4096
    jw_tok = at("jw_tok", mo, [128, NT * 96 + 64], BF16); mo += 3328
    rd_tok = at("rd_tok", mo, [128, NT * 96 + 64], BF16); mo += 3328
    FEATB = mo
    jwT, rdT, rdsT = [], [], []
    for g in range(4):
        jwT.append(at(f"jwT{g}", mo, [128, T], BF16)); mo += 4096
        rdT.append(at(f"rdT{g}", mo, [128, T], BF16)); mo += 4096
        rdsT.append(at(f"rdsT{g}", mo, [128, T], BF16)); mo += 4096
    at_sb = at("at_sb", mo, [128, 2 * 2048], BF16); mo += 8192
    rdm_sb = at("rdm_sb", mo, [128, 4 * 512], BF16); mo += 4096
    m0all = at("m0all", mo, [128, 24], F32); mo += 96
    m0d = at("m0d", mo, [128, 24], F32); mo += 96
    m0bf = at("m0bf", mo, [128, 24], BF16); mo += 64
    jwv_t = at("jwv_t", mo, [128, 24], BF16); mo += 64
    prod1 = at("prod1", mo, [128, 384], F32); mo += 1536
    prod2 = at("prod2", mo, [128, 384], F32); mo += 1536
    sigt = at("sigt", mo, [128, 256], F32); mo += 1024
    sigt2 = at("sigt2", mo, [128, 256], F32); mo += 1024
    padbuf = at("padbuf", mo, [128, 2048], BF16); mo += 4096
    assert mo <= TOP, mo
    lraw = at("lraw", FEATB, [128, NT * 96], F32)
    sall = at("sall", FEATB + 6144, [128, 512], F32)
    rall = at("rall", FEATB + 8192, [128, 512], F32)
    nwt = at("nwt", FEATB + 10240, [128, 512], F32)
    fmst = at("fmst", FEATB + 12288, [128, 2048], BF16)
    fmsh = at("fmsh", FEATB + 16384, [128, 2048], BF16)
    lraw2 = at("lraw2", FEATB + 20480, [128, NT * 96], F32)

    ao = MEMB
    qkvwin = at("qkvwin", ao, [128, 2 * 3 * 1024], BF16); ao += 12288
    qaug, kaug, vaug, ptb = [], [], [], []
    for i in range(2):
        qaug.append(at(f"qaug{i}", ao, [65, T], BF16)); ao += 4096
    for i in range(2):
        kaug.append(at(f"kaug{i}", ao, [65, T], BF16)); ao += 4096
    for i in range(2):
        vaug.append(at(f"vaug{i}", ao, [128, 16 * 66], BF16)); ao += 2112
    vt_st = at("vt_st", ao, [128, T], BF16); ao += 4096
    for i in range(2):
        ptb.append(at(f"ptb{i}", ao, [128, 1024], BF16)); ao += 2048
    oT_st = at("oT_st", ao, [65, T], BF16); ao += 4096
    sq_b = at("sq_b", ao, [128, T], BF16); ao += 4096
    SEQB = ao
    seqout = at("seqout", ao, [128, NT * 1024], BF16); ao += 32768
    assert ao <= TOP, ao

    mvout_w = at("mvout_w", MEMB, [128, 2 * 8192], BF16)
    cmbT = at("cmbT", MEMB + 32768, [128, 1024], BF16)
    ln_scrap = at("ln_scrap", MEMB + 34816, [128, 1024], F32)
    outb_row = at("outb_row", MEMB + 38912, [1, D], BF16)
    mvb_row = at("mvb_row", MEMB + 40960, [1, D], BF16)
    assert MEMB + 43008 <= SEQB

    wguT = at("wguT", CB + 65536, [128, 8 * 8192], FP8)
    wdTr = at("wdTr", CB + 131072, [128, 32 * 1024], FP8)
    fo = CB + 163840
    htile = at("htile", fo, [128, INNER], F32); fo += 16384
    hqb0 = at("hqb0", fo, [128, 1024], BF16); fo += 2048
    hqb1 = at("hqb1", fo, [128, 1024], BF16); fo += 2048
    xq_t = at("xq_t", fo, [128, 1024], BF16); fo += 2048
    xqT = at("xqT", fo, [128, 1024], BF16); fo += 2048
    xqT1 = at("xqT1", fo, [128, 1024], BF16); fo += 2048
    dsl0 = at("dsl0", fo, [128, 1024], BF16); fo += 2048
    silu_t = at("silu_t", fo, [128, 512], F32); fo += 2048
    assert fo <= TOP, fo
    sq_scr = htile[:, 3072:4096]

    prep_stage = at("prep_stage", CB, [128, 32 * 1024], F32)
    prep_w8f = at("prep_w8f", CB + 131072, [128, 32768], FP8)
    prep_w16 = prep_w8f.bitcast(BF16)
    PQ = CB + 163840
    prep_tq = at("prep_tq", PQ, [128, 1024], F32); PQ += 4096
    prep_tq2 = at("prep_tq2", PQ, [128, 1024], F32); PQ += 4096
    prep_tern = at("prep_tern", PQ, [128, 1024], BF16); PQ += 2048
    prep_tern2 = at("prep_tern2", PQ, [128, 1024], BF16); PQ += 2048
    prep_st2 = at("prep_st2", PQ, [128, 2 * 1024], BF16); PQ += 4096
    prep_scl = at("prep_scl", PQ, [128, 1], F32); PQ += 64
    prep_par = at("prep_par", PQ, [128, 4], F32); PQ += 64
    prep_abs = at("prep_abs", PQ, [128, 32], F32); PQ += 128
    brow3k = at("brow3k", PQ, [1, 3072], F32); PQ += 12288
    assert PQ <= TOP, PQ

    for s in ("pe", "ve", "sc", "gp"):
        P.sem(s)
    scratch_toks = []
    out_toks = [[], []]

    tapset = {}
    for tag, name in taps:
        tapset.setdefault(tag, []).append(name)

    def dbg_dump(tag):
        if tag not in tapset:
            return
        P.barrier()
        for name in tapset[tag]:
            ap = tens[name]
            dd = nc.dram_tensor("dbg_" + name, list(ap.shape), ap.dtype,
                                kind="ExternalOutput").ap()
            P.dbg_outs.append("dbg_" + name)
            P.dma("sy", lambda e, dd=dd, ap=ap: e.dma_start(dd, ap),
                  wait=P.dma_all())
        P.barrier()

    def emit_consts():
        dx0 = []
        dx0.append(P.dma("sy", lambda e: e.dma_start(rowa, dlog_d.unsqueeze(0))))
        dx0.append(P.dma("sy", lambda e: e.dma_start(snb, mem_scale_d.unsqueeze(0))))
        dx0.append(P.dma("sy", lambda e: e.dma_start(s4[:, 0:1], iter_mix_d.unsqueeze(0))))
        dx0.append(P.dma("sy", lambda e: e.dma_start(mgrow, b_mg.unsqueeze(0))))
        dx0.append(P.dma("sy", lambda e: e.dma_start(brow3k, b_qkv.unsqueeze(0))))

        def gpc(fn):
            P.gp(fn, wait=[("gp", P.val("gp"))], inc="gp")

        for t_ in (ident, identf, diagmask, strictm, onesrow, onesrow_f,
                   onescol_f, ident64):
            gpc(lambda e, t=t_: e.memset(t, 1.0))
        gpc(lambda e: e.affine_select(ident, ident, [[1, 128]], OP.is_equal,
                                      0.0, base=0, channel_multiplier=-1))
        gpc(lambda e: e.affine_select(identf, identf, [[1, 128]], OP.is_equal,
                                      0.0, base=0, channel_multiplier=-1))
        gpc(lambda e: e.affine_select(ident64, ident64, [[1, 64]], OP.is_equal,
                                      0.0, base=64, channel_multiplier=-1))
        gpc(lambda e: e.affine_select(diagmask, diagmask, [[1, 128]], OP.is_ge,
                                      0.0, base=0, channel_multiplier=-1))
        gpc(lambda e: e.affine_select(strictm, strictm, [[1, 128]], OP.is_ge,
                                      0.0, base=-1, channel_multiplier=-1))
        gpc(lambda e: e.memset(blockones, 0.0))
        gpc(lambda e: e.memset(blockones[0:64, 0:1], 1.0))
        gpc(lambda e: e.memset(blockones[64:128, 1:2], 1.0))
        gpc(lambda e: e.iota(prow, [[1, 128]], base=0, channel_multiplier=0,
                             allow_small_or_imprecise_dtypes=True))
        gpc(lambda e: e.iota(pcol, [[0, 1]], base=0, channel_multiplier=1,
                             allow_small_or_imprecise_dtypes=True))
        gpc(lambda e: e.iota(c128, [[0, 1]], base=128, channel_multiplier=-1,
                             allow_small_or_imprecise_dtypes=True))
        gpc(lambda e: e.memset(magiccol, MAGIC))
        gpc(lambda e: e.memset(mwcol, 0.0))
        gpc(lambda e: e.memset(c127, 127.0))
        gp0 = P.val("gp")

        P.sc(lambda e: e.activation(rowb, rowa, AF.Exp, bias=0.0, scale=-1.0),
             wait=dx0 + [("gp", gp0)], inc="sc")
        P.ve(lambda e: e.tensor_scalar(rowb, rowb, 1.0, 0.0, OP.add, OP.add),
             wait=[("sc", P.val("sc"))])
        P.ve(lambda e: e.reciprocal(rowa, rowb), inc="ve")
        P.sc(lambda e: e.activation(rowb, rowa, AF.Ln, bias=0.0, scale=1.0),
             wait=[("ve", P.val("ve"))], inc="sc")
        sc_ln = P.val("sc")
        P.ve(lambda e: e.tensor_scalar(rown, rowb, -1.0, 0.0, OP.mult, OP.add),
             wait=[("sc", sc_ln)], inc="ve")
        P.sc(lambda e: e.activation(rowc, rowb, AF.Exp, bias=0.0, scale=128.0),
             wait=[("sc", sc_ln)], inc="sc")
        sc_dc = P.val("sc")
        P.sc(lambda e: e.activation(s4[:, 1:2], s4[:, 0:1], AF.Exp, bias=0.0,
                                    scale=-1.0), wait=dx0)
        P.ve(lambda e: e.tensor_scalar(s4[:, 1:2], s4[:, 1:2], 1.0, 0.0,
                                       OP.add, OP.add), wait=[("sc", P.val("sc"))])
        P.ve(lambda e: e.reciprocal(s4[:, 1:2], s4[:, 1:2]), inc="ve")
        ve_al = P.val("ve")
        P.ve(lambda e: e.tensor_copy(mgrow_bf, mgrow), wait=dx0, inc="ve")

        def bcast(dst, src_row, n, wait):
            P.pe(lambda e, dst=dst, src=src_row, n=n: e.matmul(
                bank(7)[:, 0:n], onesrow_f, src, start=True, stop=True),
                wait=wait, inc="pe")
            P.ve(lambda e, dst=dst, n=n: e.tensor_copy(dst, bank(7)[:, 0:n]),
                 wait=[("pe", P.val("pe"))], inc="ve")
            return [("ve", P.val("ve"))]

        wv = bcast(lndbc, rowb, 16, [("sc", sc_ln), ("ve", P.val("ve")), ("gp", gp0)])
        wv = bcast(alcol, s4[:, 1:2], 1, wv + [("ve", ve_al)])
        wv = bcast(mscbc, snb, 16, wv + dx0)
        for h in range(H):
            P.sc(lambda e, h=h: e.activation(vcols[:, h:h + 1], c128, AF.Exp,
                                             bias=0.0, scale=lndbc[:, h:h + 1]),
                 wait=wv + [("gp", gp0)] if h == 0 else ())
        pass

        for g in range(4):
            for j in range(4):
                h = 4 * g + j
                P.pe(lambda e, g=g, j=j, h=h: e.matmul(
                    bank(7)[32 * j:32 * (j + 1), 64 + g:65 + g],
                    onesrow_f[:, 0:32], rowc[:, h:h + 1],
                    start=True, stop=True,
                    tile_position=(0, 32 * j)),
                    wait=wv + [("sc", sc_dc)] if (g == 0 and j == 0) else ())
        P.mark("pe")
        P.ve(lambda e: e.tensor_copy(dcolC, bank(7)[:, 64:68]),
             wait=[("pe", P.val("pe"))], inc="ve")

        P.ve(lambda e: e.memset(bank(5)[:, 0:128], 0.0),
             wait=[("pe", P.val("pe"))], inc="ve")
        for h in range(H):
            g, j = h // 4, h % 4
            P.sc(lambda e, h=h: e.activation(arow_b, prow, AF.Exp, bias=0.0,
                                             scale=rowb[:, h:h + 1]),
                 wait=[("ve", P.val("ve")), ("pe", P.val("pe"))])
            P.sc(lambda e, h=h: e.activation(brow_b, prow, AF.Exp, bias=0.0,
                                             scale=rown[:, h:h + 1]), inc="sc")
            sc_ab = P.val("sc")
            P.pe(lambda e: e.matmul(bank(6)[:, 0:128], brow_b, arow_b,
                                    start=True, stop=True),
                 wait=[("sc", sc_ab)], inc="pe")
            P.ve(lambda e, h=h: e.tensor_tensor(
                dmats[:, 128 * h:128 * (h + 1)], bank(6)[:, 0:128], strictm,
                OP.mult), wait=[("pe", P.val("pe"))], inc="ve")
            P.pe(lambda e, j=j: e.matmul(bank(5)[32 * j:32 * j + 6, 0:128],
                                         onesrow_f[:, 0:6], arow_b,
                                         start=True, stop=True,
                                         tile_position=(0, 32 * j)),
                 wait=[("sc", sc_ab)], inc="pe")
            if j == 3:
                P.ve(lambda e, g=g: e.tensor_copy(
                    scaleg[:, 128 * g:128 * (g + 1)], bank(5)[:, 0:128]),
                    wait=[("pe", P.val("pe"))], inc="ve")

        for oc in range(24):
            P.pe(lambda e, oc=oc: e.transpose(
                bank(4)[:, oc:oc + 1], brow3k[:, 128 * oc:128 * (oc + 1)],
                identf[0:1, 0:1]),
                wait=dx0 + [("ve", P.val("ve"))] if oc == 0 else ())
        P.mark("pe")
        P.ve(lambda e: e.tensor_copy(bcolq, bank(4)[:, 0:24]),
             wait=[("pe", P.val("pe"))])
        P.ve(lambda e: e.tensor_scalar(bcolq[:, 0:8], bcolq[:, 0:8], QKV_SCALE,
                                       0.0, OP.mult, OP.add), inc="ve")
        P.pe(lambda e: e.transpose(bank(4)[0:16, 32:33], mgrow, identf[0:1, 0:1]),
             wait=[("ve", P.val("ve"))], inc="pe")
        P.ve(lambda e: e.tensor_copy(mgbcol, bank(4)[0:16, 32:33]),
             wait=[("pe", P.val("pe"))], inc="ve")
        P.barrier()

    def emit_prep_small():
        wlT3 = wlT.rearrange("p (dc f) -> p dc f", f=272)
        dwv = []
        dwv.append(P.dma("gp", lambda e: e.dma_start(prep_st2[0:64, 0:1024], w_wl["w1"])))
        dwv.append(P.dma("gp", lambda e: e.dma_start(prep_st2[64:128, 0:1024], w_wl["w2"])))
        dwv.append(P.dma("gp", lambda e: e.dma_start(prep_st2[0:64, 1024:2048], w_wl["r1"])))
        dwv.append(P.dma("gp", lambda e: e.dma_start(prep_st2[64:128, 1024:2048], w_wl["r2"])))
        for sect in range(2):
            for dc in range(8):
                P.pe(lambda e, dc=dc, sect=sect: e.transpose(
                    bankb(sect)[:, 128 * dc:128 * (dc + 1)],
                    prep_st2[:, 1024 * sect + 128 * dc:1024 * sect + 128 * (dc + 1)],
                    ident), wait=dwv if dc == 0 else ())
            P.mark("pe")
            P.ve(lambda e, sect=sect: e.tensor_copy(
                wlT3[:, :, 128 * sect:128 * (sect + 1)],
                bankb(sect).rearrange("p (dc f) -> p dc f", f=128)),
                wait=[("pe", P.val("pe"))], inc="ve")
        dwv = [P.dma("gp", lambda e: e.dma_start(prep_st2[0:16, 0:1024], w_mg),
                     wait=[("ve", P.val("ve"))])]
        for dc in range(8):
            P.pe(lambda e, dc=dc: e.transpose(
                bankb(2)[:, 16 * dc:16 * (dc + 1)],
                prep_st2[0:16, 128 * dc:128 * (dc + 1)], ident[0:16, 0:16]),
                wait=dwv if dc == 0 else ())
        P.mark("pe")
        P.ve(lambda e: e.tensor_copy(
            wlT3[:, :, 256:272],
            bankb(2).rearrange("p (dc f) -> p dc f", f=16)[:, 0:8, :]),
            wait=[("pe", P.val("pe"))], inc="ve")

    def emit_prep_big():
        secs = [(w_qkv, 0, True, qkvT_s, 0), (w_qkv, 8, False, qkvT_s, 8192),
                (w_qkv, 16, False, qkvT_s, 16384),
                (w_mv, 0, False, mvT_s, 0), (w_out, 0, False, outT_s, 0)]
        last_tr = {0: P.val("pe"), 1: P.val("pe")}
        last_ev = {0: P.val("ve"), 1: P.val("ve")}
        w16_free = []
        for dram, rc0, isq, scr, scol in secs:
            for rc in range(8):
                buf = rc % 2
                dst = prep_st2[:, 1024 * buf:1024 * (buf + 1)]
                dwv = [P.dma("gp", lambda e, d=dram, r=rc0 + rc, dst=dst:
                             e.dma_start(dst, d[128 * r:128 * (r + 1), :]),
                             wait=[("pe", last_tr[buf])])]
                bb = rc % 2
                for dc in range(8):
                    P.pe(lambda e, dc=dc, bb=bb, dst=dst: e.transpose(
                        bankb(bb)[:, 128 * dc:128 * (dc + 1)],
                        dst[:, 128 * dc:128 * (dc + 1)], ident),
                        wait=dwv + [("ve", last_ev[bb])] if dc == 0 else ())
                P.mark("pe")
                last_tr[buf] = P.val("pe")
                pev = P.val("pe")
                if scr is qkvT_s:
                    dsl = prep_w16[:, 1024 * rc:1024 * (rc + 1)]
                    src = bankb(bb)
                else:
                    dsl = prep_w16[:, 0:8192].rearrange(
                        "p (dc f) -> p dc f", f=1024)[:, :, 128 * rc:128 * (rc + 1)]
                    src = bankb(bb).rearrange("p (dc f) -> p dc f", f=128)
                wt = [("pe", pev)] + (w16_free if rc == 0 else [])
                if isq:
                    P.ve(lambda e, dsl=dsl, src=src: e.tensor_scalar(
                        dsl, src, QKV_SCALE, 0.0, OP.mult, OP.add),
                        wait=wt, inc="ve")
                else:
                    P.ve(lambda e, dsl=dsl, src=src: e.tensor_copy(dsl, src),
                         wait=wt, inc="ve")
                last_ev[bb] = P.val("ve")
            vev = P.val("ve")
            w16_free = [P.dma("sy", lambda e, scr=scr, scol=scol: e.dma_start(
                scr[:, scol:scol + 8192], prep_w16[:, 0:8192]),
                wait=[("ve", vev)])]
            scratch_toks.extend(w16_free)

    PREP_SPECS = [(w_g, 32, 1024, "g"), (w_u, 32, 1024, "u"),
                  (w_dn, 8, 4096, "d")]
    prep_stage_free = {k: [] for k in range(4)}
    prep_dma_toks = {}

    def emit_prep_ffn_dma(kind):
        dram, nrc, ccols, _ = next(s for s in PREP_SPECS if s[3] == kind)
        rpc = nrc // 8
        toks = {k: [] for k in range(4)}
        for hk in range(8):
            k = hk // 2
            dst = prep_stage[:, 4096 * hk:4096 * (hk + 1)].rearrange(
                "p (r c) -> p r c", c=ccols)
            srcd = dram[128 * rpc * hk:128 * rpc * (hk + 1), :].rearrange(
                "(r p) c -> p r c", p=128)
            toks[k].append(P.dma_big(hk, lambda e, dst=dst, srcd=srcd:
                                     e.dma_start(dst, srcd),
                                     wait=prep_stage_free[k]))
        prep_dma_toks[kind] = toks

    def emit_prep_ffn():
        w8_free = list(scratch_toks)
        for wx, (dram, nrc, ccols, kind) in enumerate(PREP_SPECS):
            dwv = prep_dma_toks[kind]
            for k in range(4):
                P.ve(lambda e, k=k: e.reduce_sum(
                    prep_par[:, k:k + 1],
                    prep_stage[:, 8192 * k:8192 * (k + 1)], axis=AX,
                    apply_absolute_value=True),
                    wait=dwv[k], inc=("ve" if k == 3 else None))
            P.ve(lambda e: e.tensor_reduce(prep_scl, prep_par, AX, OP.add))
            P.pe(lambda e: e.matmul(bank(4)[0:1, 0:1], prep_scl, onescol_f,
                                    start=True, stop=True),
                 wait=[("ve", P.val("ve"))], inc="pe")
            P.ve(lambda e: e.tensor_scalar(s4[:, 0:1], bank(4)[0:1, 0:1],
                                           1024.0 / (1 << 22), 0.0,
                                           OP.mult, OP.add),
                 wait=[("pe", P.val("pe"))], inc="ve")
            P.pe(lambda e: e.matmul(bank(4)[:, 3:4], onesrow_f, s4[:, 0:1],
                                    start=True, stop=True),
                 wait=[("ve", P.val("ve"))], inc="pe")
            P.ve(lambda e: e.tensor_copy(prep_scl, bank(4)[:, 3:4]),
                 wait=[("pe", P.val("pe"))], inc="ve")
            m1tok = P.val("ve")
            for blk in range(32):
                P.sc(lambda e, blk=blk: e.activation(
                    prep_tq, prep_stage[:, 1024 * blk:1024 * (blk + 1)],
                    AF.Abs, bias=0.0, scale=1.0,
                    accum_out=prep_abs[:, blk:blk + 1]),
                    wait=([("ve", m1tok)] if blk == 0 else ()), inc="sc")
            P.ve(lambda e: e.tensor_tensor(
                prep_abs, prep_abs,
                bass.AP(prep_scl.tensor, prep_scl.offset,
                        [list(prep_scl.ap[0]), [0, 32]]), OP.subtract),
                 wait=[("sc", P.val("sc"))])
            P.ve(lambda e: e.tensor_reduce(prep_par[:, 0:1], prep_abs, AX,
                                           OP.add), inc="ve")
            P.pe(lambda e: e.matmul(bank(4)[0:1, 4:5], prep_par[:, 0:1],
                                    onescol_f, start=True, stop=True),
                 wait=[("ve", P.val("ve"))], inc="pe")
            P.ve(lambda e: e.tensor_copy(s4[:, 1:2], bank(4)[0:1, 4:5]),
                 wait=[("pe", P.val("pe"))])
            P.ve(lambda e: e.tensor_tensor(s4[:, 2:3], s4[:, 1:2],
                                           bank(4)[0:1, 0:1], OP.add),
                 inc="ve")
            P.ve(lambda e, n=nrc, c=ccols: e.tensor_scalar(
                s4[:, 4:5], s4[:, 2:3], 1.0 / (n * 128 * c), 1e-5,
                OP.mult, OP.max))
            P.ve(lambda e: e.reciprocal(s4[:, 5:6], s4[:, 4:5]), inc="ve")
            P.pe(lambda e: e.matmul(bank(4)[:, 1:2], onesrow_f, s4[:, 5:6],
                                    start=True, stop=True),
                 wait=[("ve", P.val("ve"))])
            P.pe(lambda e: e.matmul(bank(4)[:, 2:3], onesrow_f, s4[:, 4:5],
                                    start=True, stop=True), inc="pe")
            wi = {"g": 0, "u": 1, "d": 2}[kind]
            P.ve(lambda e: e.tensor_copy(prep_scl, bank(4)[:, 1:2]),
                 wait=[("pe", P.val("pe"))])
            P.ve(lambda e, wi=wi: e.tensor_copy(mwcol[:, wi:wi + 1],
                                                bank(4)[:, 2:3]), inc="ve")
            vescale = P.val("ve")

            npieces = nrc * ccols // 1024
            last_tr = {0: P.val("pe"), 1: P.val("pe")}
            last_ev = {2: ("ve", P.val("ve")), 3: ("ve", P.val("ve"))}
            terns = [prep_tern, prep_tern2]
            tqs = [prep_tq, prep_tq2]
            tq_free = {0: P.val("ve"), 1: P.val("ve")}
            pend_ev = [None]
            stage_done = {}
            for piece in range(npieces):
                src = prep_stage[:, 1024 * piece:1024 * (piece + 1)]
                trn = terns[piece % 2]
                tq = tqs[piece % 2]
                P.sc(lambda e, src=src, tq=tq: e.activation(
                    tq, src, AF.Identity, bias=magiccol, scale=prep_scl),
                    wait=([("ve", vescale)] if piece == 0 else
                          [("ve", tq_free[piece % 2])]), inc="sc")
                scv = P.val("sc")
                if piece % 8 == 7:
                    stage_done[piece // 8] = scv
                P.ve(lambda e, tq=tq: e.tensor_scalar(
                    tq, tq, MAGIC + 1.0, -MAGIC, OP.min, OP.add),
                     wait=[("sc", scv)])
                P.ve(lambda e, trn=trn, tq=tq: e.tensor_scalar(
                    trn, tq, -1.0, 1.0, OP.max, OP.mult),
                     wait=[("pe", last_tr[piece % 2])], inc="ve")
                tq_free[piece % 2] = P.val("ve")
                if pend_ev[0] is not None:
                    pend_ev[0]()
                    pend_ev[0] = None
                bb = 2 + piece % 2
                for dc in range(8):
                    P.pe(lambda e, dc=dc, bb=bb, trn=trn: e.transpose(
                        bankb(bb)[:, 128 * dc:128 * (dc + 1)],
                        trn[:, 128 * dc:128 * (dc + 1)], ident),
                        wait=[("ve", tq_free[piece % 2]),
                              last_ev[bb]] if dc == 0 else ())
                P.mark("pe")
                last_tr[piece % 2] = P.val("pe")
                pev = P.val("pe")
                if kind in ("g", "u"):
                    rc = piece
                    dsl = prep_w8f.rearrange("p (dc f) -> p dc f", f=4096)[
                        :, :, 128 * rc:128 * (rc + 1)]
                else:
                    rc, icq = piece // 4, piece % 4
                    dsl = prep_w8f.rearrange("p (icq ic f) -> p icq ic f",
                                             icq=4, ic=8)[
                        :, icq, :, 128 * rc:128 * (rc + 1)]
                wt = [("pe", pev)] + (w8_free if piece == 0 else [])

                def mk_ev(bb=bb, dsl=dsl, wt=wt):
                    def em():
                        P.ve(lambda e, bb=bb, dsl=dsl: e.tensor_copy(
                            dsl, bankb(bb).rearrange("p (c f) -> p c f",
                                                     f=128)),
                            wait=wt, inc="ve")
                        last_ev[bb] = ("ve", P.val("ve"))
                    return em
                pend_ev[0] = mk_ev()
            if pend_ev[0] is not None:
                pend_ev[0]()
                pend_ev[0] = None
            gev = P.val("ve")
            if kind == "g":
                tok = P.dma("sy", lambda e: e.dma_start(
                    wguT_s.rearrange("p (dc f) -> p dc f", f=8192)[:, :, 0:4096],
                    prep_w8f.rearrange("p (dc f) -> p dc f", f=4096)),
                    wait=[("ve", gev)])
            elif kind == "u":
                tok = P.dma("sy", lambda e: e.dma_start(
                    wguT_s.rearrange("p (dc f) -> p dc f", f=8192)[:, :, 4096:8192],
                    prep_w8f.rearrange("p (dc f) -> p dc f", f=4096)),
                    wait=[("ve", gev)])
            else:
                tok = P.dma("sy", lambda e: e.dma_start(wdT_s, prep_w8f),
                            wait=[("ve", gev)])
            w8_free = [tok]
            scratch_toks.append(tok)
            for k in range(4):
                prep_stage_free[k] = [("sc", stage_done[k])]
            if wx + 1 < len(PREP_SPECS):
                emit_prep_ffn_dma(PREP_SPECS[wx + 1][3])

    def emit_p0(b):
        dxv = [P.dma("sy", lambda e, b=b: e.dma_start(
            xa.rearrange("p (t d) -> p t d", d=1024),
            x_d[b].rearrange("(t q) d -> q t d", q=128)),
            wait=P.dma_all() if b else [])]
        ev_guard = {0: P.val("ve"), 1: P.val("ve")}
        for dc in range(8):
            for tq in range(4):
                bk = (dc * 4 + tq) % 2
                for ti in range(4):
                    t = tq * 4 + ti
                    w = dxv + [("ve", ev_guard[bk])] if ti == 0 else ()
                    P.pe(lambda e, dc=dc, t=t, bk=bk, ti=ti: e.transpose(
                        bank(bk)[:, 128 * ti:128 * (ti + 1)],
                        xa[:, 1024 * t + 128 * dc:1024 * t + 128 * (dc + 1)],
                        identf), wait=w)
                P.mark("pe")
                pev = P.val("pe")
                P.ve(lambda e, dc=dc, tq=tq, bk=bk: e.tensor_copy(
                    xb[:, 2048 * dc + 512 * tq:2048 * dc + 512 * (tq + 1)],
                    bank(bk)), wait=[("pe", pev)], inc="ve")
                ev_guard[bk] = P.val("ve")

    PAIRS = [(0, 1), (0, 2), (0, 3), (1, 2), (1, 3), (2, 3)]

    def wsl(tok, half, i):
        return bass.AP(tok.tensor, tok.offset + 64 * half + i,
                       [list(tok.ap[0]), [128, NT], [4, 16]])

    def lsl(k):
        return bass.AP(lraw.tensor, lraw.offset + k,
                       [list(lraw.ap[0]), [96, NT], [6, 16]])

    def ssl(soff):
        return bass.AP(sall.tensor, sall.offset + soff,
                       [list(sall.ap[0]), [16, NT], [1, 16]])

    def l2c():
        return bass.AP(lraw2.tensor, lraw2.offset,
                       [list(lraw2.ap[0]), [16, NT], [1, 16]])

    def jsl(dst, c0, cnt, cstep):
        return bass.AP(dst.tensor, dst.offset + c0,
                       [list(dst.ap[0]), [96, NT], [6, 16], [cstep, cnt]])

    def lsl2(k0, cnt, kstep):
        return bass.AP(lraw.tensor, lraw.offset + k0,
                       [list(lraw.ap[0]), [96, NT], [6, 16], [kstep, cnt]])

    def rbc(soff, cnt):
        return bass.AP(rall.tensor, rall.offset + soff,
                       [list(rall.ap[0]), [16, NT], [1, 16], [0, cnt]])

    def emit_p2():
        for sect, tok_dst in ((0, w12_tok), (1, r12_tok)):
            for nb in range(4):
                for dc in range(8):
                    w = [("ve", P.val("ve"))] if (nb == 0 and dc == 0) else ()
                    P.pe(lambda e, nb=nb, dc=dc, sect=sect: e.matmul(
                        bank(nb),
                        wlT[:, 272 * dc + 128 * sect:272 * dc + 128 * (sect + 1)],
                        xb[:, 2048 * dc + 512 * nb:2048 * dc + 512 * (nb + 1)],
                        start=(dc == 0), stop=(dc == 7)), wait=w)
            P.mark("pe")
            pev = P.val("pe")
            P.ve(lambda e: e.tensor_copy(fmst, PS[:, 0:2048]),
                 wait=[("pe", pev)], inc="ve")
            if sect == 0:
                P.ve(lambda e: e.tensor_copy(fmsh[0:64, 1:2048],
                                             fmst[0:64, 0:2047]))
                P.ve(lambda e: e.memset(fmsh[0:64, 0:1], 0.0))
                P.ve(lambda e: e.tensor_copy(fmst[0:64, :], fmsh[0:64, :]),
                     inc="ve")
            vev = P.val("ve")
            evg = {4: P.val("ve"), 5: P.val("ve")}
            for tq in range(4):
                bb = 4 + tq % 2
                for ti in range(4):
                    t = tq * 4 + ti
                    w = [("ve", max(vev, evg[bb]))] if ti == 0 else ()
                    P.pe(lambda e, t=t, ti=ti, bb=bb: e.transpose(
                        bankb(bb)[:, 256 * ti:256 * ti + 128],
                        fmst[:, 128 * t:128 * (t + 1)], ident), wait=w)
                P.mark("pe")
                pev = P.val("pe")
                P.ve(lambda e, tq=tq, bb=bb, tok_dst=tok_dst: e.tensor_copy(
                    tok_dst.rearrange("p (t f) -> p t f", f=128)[
                        :, 4 * tq:4 * (tq + 1), :],
                    bankb(bb).rearrange("p (t f) -> p t f", f=256)[:, :, 0:128]),
                    wait=[("pe", pev)], inc="ve")
                evg[bb] = P.val("ve")

        vev = P.val("ve")
        for t in range(NT):
            for dc in range(8):
                P.pe(lambda e, t=t, dc=dc: e.matmul(
                    bank(6)[:, 16 * t:16 * (t + 1)],
                    xb[:, 2048 * dc + 128 * t:2048 * dc + 128 * (t + 1)],
                    wlT[:, 272 * dc + 256:272 * dc + 272],
                    start=(dc == 0), stop=False),
                    wait=([("ve", vev)] if (t == 0 and dc == 0) else ()))
            P.pe(lambda e, t=t: e.matmul(
                bank(6)[:, 16 * t:16 * (t + 1)], onesrow, mgrow_bf,
                start=False, stop=True))
        P.mark("pe")
        pev = P.val("pe")
        P.sc(lambda e: e.activation(gatesig, bank(6)[:, 0:256], AF.Exp,
                                    bias=0.0, scale=-1.0),
             wait=[("pe", pev)], inc="sc")
        P.ve(lambda e: e.tensor_scalar(gatesig, gatesig, 1.0, 0.0, OP.add,
                                       OP.add), wait=[("sc", P.val("sc"))])
        P.ve(lambda e: e.reciprocal(gatesig, gatesig), inc="ve")

        P.ve(lambda e: e.memset(jw_tok, 0.0))
        P.ve(lambda e: e.memset(rd_tok, 0.0), inc="ve")
        for kindi, (tok, dst_tok) in enumerate(((w12_tok, jw_tok),
                                                (r12_tok, rd_tok))):
            for k, (i, j) in enumerate(PAIRS):
                P.ve(lambda e, tok=tok, i=i, j=j, k=k: e.tensor_tensor(
                    lsl(k), wsl(tok, 0, i), wsl(tok, 1, j), OP.mult))
                P.ve(lambda e, tok=tok, i=i, j=j: e.tensor_tensor(
                    l2c(), wsl(tok, 0, j), wsl(tok, 1, i), OP.mult))
                P.ve(lambda e, k=k: e.tensor_tensor(
                    lsl(k), lsl(k), l2c(), OP.subtract))
            soff = 256 * kindi
            P.ve(lambda e, soff=soff: e.tensor_tensor(
                ssl(soff), lsl(0), lsl(0), OP.mult))
            for k in range(1, 6):
                P.ve(lambda e, k=k: e.tensor_tensor(l2c(), lsl(k), lsl(k),
                                                    OP.mult))
                P.ve(lambda e, soff=soff: e.tensor_tensor(
                    sall[:, soff:soff + 256], sall[:, soff:soff + 256],
                    lraw2[:, 0:256], OP.add))
            P.ve(lambda e, soff=soff: e.tensor_single_scalar(
                sall[:, soff:soff + 256], sall[:, soff:soff + 256], 1e-24,
                OP.max), inc="ve")
            P.sc(lambda e, soff=soff: e.activation(
                nwt[:, soff:soff + 256], sall[:, soff:soff + 256], AF.Ln,
                bias=0.0, scale=1.0), wait=[("ve", P.val("ve"))], inc="sc")
            P.sc(lambda e, soff=soff: e.activation(
                rall[:, soff:soff + 256], nwt[:, soff:soff + 256], AF.Exp,
                bias=0.0, scale=-0.5), inc="sc")
            scv = P.val("sc")
            if kindi == 0:
                P.ve(lambda e, d=dst_tok: e.scalar_tensor_tensor(
                    jsl(d, 2, 2, 1), lsl2(3, 2, -1), 1.0, rbc(0, 2),
                    OP.mult, OP.mult), wait=[("sc", scv)])
                P.ve(lambda e, d=dst_tok: e.scalar_tensor_tensor(
                    jsl(d, 0, 2, 5), lsl2(5, 2, -5), 1.0, rbc(0, 2),
                    OP.mult, OP.mult))
                P.ve(lambda e, d=dst_tok: e.scalar_tensor_tensor(
                    jsl(d, 1, 2, 3), lsl2(4, 2, -3), -1.0, rbc(0, 2),
                    OP.mult, OP.mult), inc="ve")
            else:
                P.ve(lambda e, d=dst_tok: e.scalar_tensor_tensor(
                    jsl(d, 0, 6, 1), lsl2(0, 6, 1), 1.0, rbc(256, 6),
                    OP.mult, OP.mult), wait=[("sc", scv)], inc="ve")

    def emit_p3():
        ev_bb = {0: P.val("ve"), 1: P.val("ve")}
        P.ve(lambda e: e.memset(padbuf, 0.0), inc="ve")
        pad_rd = P.val("pe")
        for kind, (tok, dstl) in enumerate(((jw_tok, jwT), (rd_tok, rdT))):
            for cq in range(4):
                for ci in range(4):
                    c = cq * 4 + ci
                    dstp = bass.AP(padbuf.tensor, padbuf.offset + 512 * ci,
                                   [list(padbuf.ap[0]), [32, 16], [1, 6]])
                    srcp = bass.AP(tok.tensor, tok.offset + 96 * c,
                                   [list(tok.ap[0]), [6, 16], [1, 6]])
                    P.ve(lambda e, dstp=dstp, srcp=srcp: e.tensor_copy(
                        dstp, srcp),
                        wait=([("pe", pad_rd)] if ci == 0 else ()),
                        inc=("ve" if ci == 3 else None))
                padtok = P.val("ve")
                for g in range(4):
                    bb = g % 2
                    for ci in range(4):
                        w = []
                        if ci == 0:
                            w = [("ve", padtok), ("ve", ev_bb[bb])]
                        P.pe(lambda e, g=g, ci=ci, bb=bb: e.transpose(
                            bankb(bb)[:, 128 * ci:128 * (ci + 1)],
                            padbuf[:, 512 * ci + 128 * g:512 * ci + 128 * (g + 1)],
                            ident), wait=w)
                    P.mark("pe")
                    pev = P.val("pe")
                    P.ve(lambda e, g=g, cq=cq, bb=bb, dstl=dstl: e.tensor_copy(
                        dstl[g][:, 512 * cq:512 * (cq + 1)], bankb(bb)[:, 0:512]),
                        wait=[("pe", pev)], inc="ve")
                    ev_bb[bb] = P.val("ve")
                pad_rd = P.val("pe")
        for g in range(4):
            src2 = bass.AP(scaleg.tensor, scaleg.offset + 128 * g,
                           [list(scaleg.ap[0]), [0, NT], [1, 128]])
            P.ve(lambda e, g=g, src2=src2: e.tensor_tensor(
                rdsT[g].rearrange("p (c f) -> p c f", f=128),
                rdT[g].rearrange("p (c f) -> p c f", f=128),
                src2, OP.mult))
        P.ve(lambda e: None, inc="ve")
        vev_feat = P.val("ve")
        P.ve(lambda e: e.memset(m0all, 0.0))
        P.ve(lambda e: e.memset(m0bf, 0.0))
        P.ve(lambda e: e.memset(bank(4, 2), 0.0))
        P.ve(lambda e: e.memset(bank(6), 0.0), inc="ve")
        ve_init = P.val("ve")
        m0_ve = {g: ve_init for g in range(4)}
        a_free = {0: vev_feat, 1: vev_feat}
        a_slot = {0: P.val("pe"), 1: P.val("pe")}
        rdm_free = {0: ve_init, 1: ve_init}
        last_upd = [P.val("pe")]
        aev = {}
        rdm_ev = {}
        for cb in range(4):
            for g in range(4):
                par = g % 2
                w0 = [("ve", a_free[0])]
                for cc in range(4):
                    c = cb * 4 + cc
                    for j in range(4):
                        P.pe(lambda e, g=g, c=c, j=j, cc=cc: e.matmul(
                                 PS[:, 512 * j + 128 * cc:512 * j + 128 * (cc + 1)],
                                 jwT[g][32 * j:32 * j + 6, 128 * c:128 * (c + 1)],
                                 rdT[g][32 * j:32 * j + 6, 128 * c:128 * (c + 1)],
                                 start=True, stop=True, tile_position=(32 * j, 0)),
                             wait=w0 if (cc == 0 and j == 0) else ())
                P.mark("pe")
                mm1tok = P.val("pe")
                dm = bass.AP(dmats.tensor, dmats.offset + 128 * 4 * g,
                             [list(dmats.ap[0]), [128, 4], [0, 4], [1, 128]])
                src = bass.AP(PS.tensor, PS.offset,
                              [list(PS.ap[0]), [512, 4], [128, 4], [1, 128]])
                dst = bass.AP(at_sb.tensor, at_sb.offset + 2048 * par,
                              [list(at_sb.ap[0]), [512, 4], [128, 4], [1, 128]])
                P.ve(lambda e, dst=dst, src=src, dm=dm: e.tensor_tensor(
                    dst, src, dm, OP.mult),
                    wait=[("pe", mm1tok), ("pe", a_slot[par])], inc="ve")
                a_free[0] = P.val("ve")
                aev[g] = P.val("ve")
                rb = 4 + par
                for cc in range(4):
                    c = cb * 4 + cc
                    for j in range(4):
                        h = 4 * g + j
                        P.ve(lambda e, g=g, c=c, j=j, h=h: e.tensor_scalar(
                            jwv_t[:, 6 * j:6 * (j + 1)],
                            jw_tok[:, 96 * c + 24 * g + 6 * j:
                                   96 * c + 24 * g + 6 * (j + 1)],
                            vcols[:, h:h + 1], 0.0, OP.mult, OP.add),
                            wait=([("pe", last_upd[0])] if j == 0 else ()),
                            inc=("ve" if j == 3 else None))
                    jwvtok = P.val("ve")
                    P.ve(lambda e, g=g: e.tensor_scalar(
                        m0d[:, 6 * g:6 * (g + 1)], m0all[:, 6 * g:6 * (g + 1)],
                        dcolC[:, g:g + 1], 0.0, OP.mult, OP.add), inc="ve")
                    m0dtok = P.val("ve")
                    for j in range(4):
                        P.pe(lambda e, g=g, c=c, j=j, cc=cc, rb=rb: e.matmul(
                            bank(rb)[:, cc * 128 + 32 * j:cc * 128 + 32 * j + 6],
                            at_sb[:, 2048 * (g % 2) + 512 * j + 128 * cc:
                                  2048 * (g % 2) + 512 * j + 128 * (cc + 1)],
                            jw_tok[:, 96 * c + 24 * g + 6 * j:
                                   96 * c + 24 * g + 6 * (j + 1)],
                            start=True, stop=False),
                            wait=([("ve", aev[g]), ("ve", rdm_free[par])]
                                  if (cc == 0 and j == 0) else ()))
                        P.pe(lambda e, g=g, c=c, j=j, cc=cc, rb=rb: e.matmul(
                            bank(rb)[:, cc * 128 + 32 * j:cc * 128 + 32 * j + 6],
                            rdsT[g][32 * j:32 * j + 6, 128 * c:128 * (c + 1)],
                            m0bf[32 * j:32 * j + 6, 6 * g:6 * (g + 1)],
                            start=False, stop=True,
                            tile_position=(32 * j, 0)),
                            wait=([("ve", m0_ve[g])] if j == 0 else ()))
                    for j in range(4):
                        P.pe(lambda e, g=g, c=c, j=j, cc=cc: e.matmul(
                            bank(6)[32 * j:32 * j + 6,
                                    128 * g + 32 * (cc % 2):128 * g + 32 * (cc % 2) + 6],
                            jwv_t[:, 6 * j:6 * (j + 1)],
                            jw_tok[:, 96 * c + 24 * g + 6 * j:
                                   96 * c + 24 * g + 6 * (j + 1)],
                            start=True, stop=True,
                            tile_position=(0, 32 * j)),
                            wait=([("ve", jwvtok), ("ve", m0_ve[g])] if j == 0
                                  else ()))
                    P.mark("pe")
                    updtok = P.val("pe")
                    last_upd[0] = updtok
                    P.ve(lambda e, g=g, cc=cc: e.tensor_tensor(
                        m0all[:, 6 * g:6 * (g + 1)],
                        bank(6)[:, 128 * g + 32 * (cc % 2):
                                128 * g + 32 * (cc % 2) + 6],
                        m0d[:, 6 * g:6 * (g + 1)], OP.add),
                        wait=[("pe", updtok), ("ve", m0dtok)])
                    P.ve(lambda e, g=g: e.tensor_copy(
                        m0bf[:, 6 * g:6 * (g + 1)], m0all[:, 6 * g:6 * (g + 1)]),
                        inc="ve")
                    m0_ve[g] = P.val("ve")
                P.mark("pe")
                a_slot[par] = P.val("pe")
                mm23tok = P.val("pe")
                P.ve(lambda e, g=g, rb=rb: e.tensor_copy(
                    rdm_sb[:, 512 * g:512 * (g + 1)], bank(rb)),
                    wait=[("pe", mm23tok)], inc="ve")
                rdm_free[par] = P.val("ve")
                rdm_ev[g] = P.val("ve")
            for g in range(4):
                rdmv = bass.AP(rdm_sb.tensor, rdm_sb.offset + 512 * g,
                               [list(rdm_sb.ap[0]), [128, 4], [32, 4], [1, 6]])
                rdv = bass.AP(rd_tok.tensor,
                              rd_tok.offset + 96 * (cb * 4) + 24 * g,
                              [list(rd_tok.ap[0]), [96, 4], [6, 4], [1, 6]])
                p1v = bass.AP(prod1.tensor, prod1.offset + 24 * g,
                              [list(prod1.ap[0]), [96, 4], [6, 4], [1, 6]])
                p2v = bass.AP(prod2.tensor, prod2.offset + 24 * g,
                              [list(prod2.ap[0]), [96, 4], [6, 4], [1, 6]])
                r1 = bass.AP(sigt.tensor, sigt.offset + 4 * g,
                             [list(sigt.ap[0]), [16, 4], [1, 4]])
                r2 = bass.AP(sigt2.tensor, sigt2.offset + 4 * g,
                             [list(sigt2.ap[0]), [16, 4], [1, 4]])
                P.ve(lambda e, rdmv=rdmv, rdv=rdv, p1v=p1v: e.tensor_tensor(
                    p1v, rdmv, rdv, OP.mult))
                P.ve(lambda e, rdmv=rdmv, p2v=p2v: e.tensor_tensor(
                    p2v, rdmv, rdmv, OP.mult))
                P.ve(lambda e, p1v=p1v, r1=r1: e.tensor_reduce(
                    r1, p1v, AX, OP.add))
                P.ve(lambda e, p2v=p2v, r2=r2: e.tensor_reduce(
                    r2, p2v, AX, OP.add))
            msv1 = bass.AP(ms_tok.tensor, ms_tok.offset + 64 * cb + 64 * 0,
                           [list(ms_tok.ap[0]), [16, 4], [4, 4], [1, 4]])
            P.ve(lambda e: e.tensor_tensor(sigt2[:, 0:64], sigt2[:, 0:64],
                                           sigt[:, 0:64], OP.subtract))
            P.ve(lambda e, msv1=msv1, cb=cb: e.scalar_tensor_tensor(
                bass.AP(ms_tok.tensor, ms_tok.offset + 64 * cb,
                        [list(ms_tok.ap[0]), [1, 64]]),
                sigt2[:, 0:64], alcol, sigt[:, 0:64], OP.mult, OP.add),
                inc="ve")

        P.ve(lambda e: e.tensor_tensor(
            sigt.rearrange("p (t h) -> p t h", h=16),
            ms_tok.rearrange("p (t h) -> p t h", h=16),
            bass.AP(mscbc.tensor, mscbc.offset,
                    [list(mscbc.ap[0]), [0, NT], [1, 16]]), OP.mult), inc="ve")
        P.sc(lambda e: e.activation(sigt2[:, 0:256], sigt[:, 0:256], AF.Exp,
                                    bias=0.0, scale=-1.0),
             wait=[("ve", P.val("ve"))], inc="sc")
        P.ve(lambda e: e.tensor_scalar(sigt2[:, 0:256], sigt2[:, 0:256], 1.0,
                                       0.0, OP.add, OP.add),
             wait=[("sc", P.val("sc"))])
        P.ve(lambda e: e.reciprocal(sigt2[:, 0:256], sigt2[:, 0:256]))
        P.ve(lambda e: e.tensor_tensor(sigt[:, 0:256], sigt2[:, 0:256],
                                       gatesig, OP.mult))
        P.ve(lambda e: e.tensor_reduce(
            bass.AP(gcols.tensor, gcols.offset,
                    [list(gcols.ap[0]), [1, NT]]),
            bass.AP(sigt.tensor, sigt.offset,
                    [list(sigt.ap[0]), [16, NT], [1, 16]]), AX, OP.add))
        P.ve(lambda e: e.tensor_single_scalar(gcols, gcols, 1.0 / 16.0,
                                              OP.mult), inc="ve")

    def emit_p4():
        for i in range(2):
            P.ve(lambda e, i=i: e.memset(kaug[i][64:65, :], -1.0))
            P.ve(lambda e, i=i: e.memset(
                vaug[i].rearrange("p (k c) -> p k c", c=66)[:, :, 64:65], 1.0),
                inc="ve")
        obank_free = P.val("ve")
        win_free = {0: P.val("pe"), 1: P.val("pe")}
        exp_g = {0: ("sc", P.val("sc")), 1: ("sc", P.val("sc"))}
        pv_g = {0: P.val("pe"), 1: P.val("pe")}
        otrans_ev = P.val("ve")
        stab_rd = {0: P.val("pe"), 1: P.val("pe")}
        gcnt = 0
        dqv = {}
        for p in range(8):
            buf = p % 2
            toks = []
            for pi, proj in enumerate((0, 1, 2)):
                w = [("pe", win_free[buf])] if pi == 0 else []
                if p == 0 and pi == 0:
                    w.extend(scratch_toks)
                toks.append(P.dma("sy", lambda e, p=p, buf=buf, proj=proj:
                                  e.dma_start(
                    qkvwin[:, 3072 * buf + 1024 * proj:3072 * buf + 1024 * (proj + 1)],
                    qkvT_s[:, 8192 * proj + 1024 * p:8192 * proj + 1024 * (p + 1)]),
                    wait=w))
            dqv[p] = toks
            evac_toks = {}
            for proj in range(3):
                for nb in range(4):
                    for dc in range(8):
                        w = []
                        if proj == 0 and nb == 0 and dc == 0:
                            w = dqv[p] + [("ve", obank_free)]
                        elif nb == 0 and dc == 0:
                            w = list(evac_toks[proj - 1])
                        P.pe(lambda e, proj=proj, nb=nb, dc=dc, buf=buf:
                             e.matmul(
                                 bank(nb),
                                 qkvwin[:, 3072 * buf + 1024 * proj + 128 * dc:
                                        3072 * buf + 1024 * proj + 128 * (dc + 1)],
                                 xb[:, 2048 * dc + 512 * nb:
                                    2048 * dc + 512 * (nb + 1)],
                                 start=(dc == 0), stop=(dc == 7)), wait=w)
                P.mark("pe")
                pev = P.val("pe")
                if proj < 2:
                    dstl = qaug if proj == 0 else kaug
                    bcol = p if proj == 0 else 8 + p
                    sl = bcolq[0:64, bcol:bcol + 1]
                    bc = bass.AP(sl.tensor, sl.offset,
                                 [list(sl.ap[0]), [0, 2048]])
                    P.ve(lambda e, dstl=dstl, bc=bc: e.tensor_tensor(
                        dstl[0][0:64, :], PS[0:64, 0:2048], bc, OP.add),
                        wait=[("pe", pev)], inc="ve")
                    P.sc(lambda e, dstl=dstl, bcol=bcol: e.activation(
                        dstl[1][0:64, :], PS[64:128, 0:2048],
                        AF.Identity, bias=bcolq[64:128, bcol:bcol + 1],
                        scale=1.0), wait=[("pe", pev)], inc="sc")
                    evac_toks[proj] = (("ve", P.val("ve")),
                                       ("sc", P.val("sc")))
                else:
                    sl = bcolq[:, 16 + p:17 + p]
                    bc = bass.AP(sl.tensor, sl.offset,
                                 [list(sl.ap[0]), [0, 2048]])
                    P.ve(lambda e, bc=bc: e.tensor_tensor(
                        vt_st, PS[:, 0:2048], bc, OP.add),
                        wait=[("pe", pev)], inc="ve")
                    evac_toks[proj] = (("ve", P.val("ve")),)
            win_free[buf] = P.val("pe")
            vevac = evac_toks[2]

            for i in range(2):
                bb = 6 + i
                for kc in range(16):
                    idn = ident[0:64, 0:64] if i == 0 else ident64[64:128, :]
                    w = ()
                    if kc == 0:
                        w = list(vevac) + [("ve", P.val("ve"))]
                        if i == 0:
                            w.append(exp_g[1])
                    P.pe(lambda e, i=i, kc=kc, bb=bb, idn=idn: e.transpose(
                        bankb(bb)[:, 64 * kc:64 * (kc + 1)],
                        vt_st[64 * i:64 * (i + 1), 128 * kc:128 * (kc + 1)],
                        idn), wait=w)
                P.mark("pe")
                pev = P.val("pe")
                P.ve(lambda e, i=i, bb=bb: e.tensor_copy(
                    bass.AP(vaug[i].tensor, vaug[i].offset,
                            [list(vaug[i].ap[0]), [66, 16], [1, 64]]),
                    bankb(bb).rearrange("p (k f) -> p k f", f=64)),
                    wait=[("pe", pev)], inc="ve")
            vaug_tok = P.val("ve")

            btoks = {}
            for i in range(2):
                sqb = sq_b if i == 0 else oT_st
                so = 8 * i
                if i == 0:
                    prow = PS[96:97, 0:2048]
                    tp = (0, 96)
                else:
                    prow = PS[64:65, 0:2048]
                    tp = None
                wk = [("pe", stab_rd[i])] + list(evac_toks[1])
                if i == 1:
                    wk.append(("ve", otrans_ev))
                P.ve(lambda e, i=i, sqb=sqb: e.tensor_tensor(
                    sqb[0:64, :], kaug[i][0:64, :], kaug[i][0:64, :],
                    OP.mult), wait=wk, inc="ve")
                sqk = P.val("ve")
                w0 = [("ve", sqk)]
                if i == 0:
                    w0 += list(evac_toks[2])
                else:
                    w0 += [("ve", otrans_ev)]
                for nb in range(4):
                    P.pe(lambda e, nb=nb, sqb=sqb, prow=prow, tp=tp: e.matmul(
                        prow[:, 512 * nb:512 * (nb + 1)],
                        blockones[0:64, 0:1],
                        sqb[0:64, 512 * nb:512 * (nb + 1)],
                        start=True, stop=True, tile_position=tp),
                        wait=w0 if nb == 0 else ())
                P.mark("pe")
                P.ve(lambda e, prow=prow, so=so: e.reduce_max(
                    snb[:, so:so + 1], prow, axis=AX),
                     wait=[("pe", P.val("pe"))], inc="ve")
                mk2tok = P.val("ve")
                k2pe = P.val("pe")
                P.ve(lambda e, i=i, sqb=sqb: e.tensor_tensor(
                    sqb[0:64, :], qaug[i][0:64, :], qaug[i][0:64, :],
                    OP.mult), wait=[("pe", k2pe)] + list(evac_toks[0]),
                    inc="ve")
                sqq = P.val("ve")
                for nb in range(4):
                    P.pe(lambda e, nb=nb, sqb=sqb, prow=prow, tp=tp: e.matmul(
                        prow[:, 512 * nb:512 * (nb + 1)],
                        blockones[0:64, 0:1],
                        sqb[0:64, 512 * nb:512 * (nb + 1)],
                        start=True, stop=True, tile_position=tp),
                        wait=[("ve", sqq), ("ve", mk2tok)] if nb == 0 else ())
                P.mark("pe")
                q2tok = P.val("pe")
                stab_rd[i] = q2tok
                P.ve(lambda e, prow=prow, so=so: e.reduce_max(
                    snb[:, so + 1:so + 2], prow, axis=AX),
                     wait=[("pe", q2tok)], inc="ve")
                P.sc(lambda e, so=so: e.activation(
                    snb[:, so + 2:so + 4], snb[:, so:so + 2], AF.Ln,
                    bias=0.0, scale=1.0),
                     wait=[("ve", P.val("ve"))], inc="sc")
                P.ve(lambda e, so=so: e.tensor_tensor(
                    snb[:, so + 4:so + 5], snb[:, so + 2:so + 3],
                    snb[:, so + 3:so + 4], OP.subtract),
                     wait=[("sc", P.val("sc"))])
                P.ve(lambda e, so=so: e.tensor_tensor(
                    snb[:, so + 5:so + 6], snb[:, so + 2:so + 3],
                    snb[:, so + 3:so + 4], OP.add), inc="ve")
                P.sc(lambda e, so=so: e.activation(
                    snb[:, so + 6:so + 8], snb[:, so + 4:so + 6], AF.Exp,
                    bias=0.0, scale=0.5),
                     wait=[("ve", P.val("ve"))], inc="sc")
                P.ve(lambda e, so=so: e.tensor_scalar(
                    snb[:, so + 6:so + 8], snb[:, so + 6:so + 8], 0.5,
                    0.0, OP.mult, OP.add),
                     wait=[("sc", P.val("sc"))], inc="ve")
                cstok = P.val("ve")
                P.sc(lambda e, i=i, prow=prow, so=so: e.activation(
                    qaug[i][64:65, :], prow, AF.Identity,
                    bias=snb[:, so + 7:so + 8], scale=snb[:, so + 6:so + 7]),
                    wait=[("ve", cstok)], inc="sc")
                btoks[i] = P.val("sc")

            for i in range(2):
                h = 2 * p + i
                groups = []
                pend1 = None
                for kc in range(16):
                    ps = qpieces(128 * kc, T)
                    for j in range(0, len(ps), 2):
                        pcs = ps[j:j + 2]
                        if len(pcs) == 2:
                            groups.append([(kc, a, b, a % 1024)
                                           for (a, b) in pcs])
                        elif pend1 is None:
                            pend1 = (kc, pcs[0])
                        else:
                            kc1, (a1, b1) = pend1
                            kc2, (a2, b2) = kc, pcs[0]
                            groups.append([(kc1, a1, b1, 0),
                                           (kc2, a2, b2, 512)])
                            pend1 = None
                if pend1 is not None:
                    kc1, (a1, b1) = pend1
                    groups.append([(kc1, a1, b1, a1 % 1024)])
                first_s = True
                pend_pv = [None]

                def flush_pv():
                    if pend_pv[0] is not None:
                        pend_pv[0]()
                        pend_pv[0] = None
                for pcs in groups:
                    bp = gcnt % 2
                    base = 2048 + 1024 * bp
                    for k, (kc, a, b2, c0) in enumerate(pcs):
                        n = b2 - a
                        w = []
                        if k == 0:
                            w = [exp_g[bp]]
                            if first_s:
                                w.append(("sc", btoks[i]))
                                if i == 0:
                                    w.append(("ve", vaug_tok))
                                first_s = False
                        P.pe(lambda e, i=i, kc=kc, a=a, n=n, base=base,
                             c0=c0: e.matmul(
                            PS[:, base + c0:base + c0 + n],
                            kaug[i][:, 128 * kc:128 * (kc + 1)],
                            qaug[i][:, a:a + n], start=True, stop=True),
                            wait=w)
                    P.mark("pe")
                    stok = P.val("pe")
                    flush_pv()
                    cmin = min(c0 for _, _, _, c0 in pcs)
                    cmax = max(c0 + (b2 - a) for kc, a, b2, c0 in pcs)
                    P.sc(lambda e, bp=bp, base=base, cmin=cmin, cmax=cmax:
                         e.activation(ptb[bp][:, cmin:cmax],
                                      PS[:, base + cmin:base + cmax],
                                      AF.Exp, bias=0.0, scale=1.0),
                         wait=[("pe", stok), ("pe", pv_g[bp])], inc="sc")
                    exp_g[bp] = ("sc", P.val("sc"))
                    mtok = None
                    for kc, a, b2, c0 in pcs:
                        if a == 128 * kc:
                            P.ve(lambda e, bp=bp, c0=c0: e.tensor_tensor(
                                ptb[bp][:, c0:c0 + 128],
                                ptb[bp][:, c0:c0 + 128],
                                diagmask, OP.mult), wait=[exp_g[bp]],
                                inc="ve")
                            mtok = ("ve", P.val("ve"))

                    def mk_pv(i=i, pcs=pcs, bp=bp, eg=exp_g[bp], mtok=mtok):
                        def em():
                            for kc, a, b2, c0 in pcs:
                                n = b2 - a
                                bkidx = a // 512
                                last_kc = min(15, 4 * bkidx + 3)
                                w2 = [eg]
                                if mtok is not None and a == 128 * kc:
                                    w2.append(mtok)
                                if kc == 0:
                                    w2.append(("ve", otrans_ev))
                                    if i == 0:
                                        w2.append(("sc", btoks[1]))
                                P.pe(lambda e, i=i, kc=kc, a=a, n=n, bp=bp,
                                     c0=c0, last_kc=last_kc:
                                     e.matmul(PS[0:65, a:a + n],
                                              vaug[i][:, 66 * kc:66 * kc + 65],
                                              ptb[bp][:, c0:c0 + n],
                                              start=(kc == 0),
                                              stop=(kc == last_kc)),
                                     wait=w2)
                            P.mark("pe")
                            pv_g[bp] = P.val("pe")
                        return em
                    pend_pv[0] = mk_pv()
                    gcnt += 1
                flush_pv()
                pvall = P.val("pe")
                P.ve(lambda e: e.tensor_copy(oT_st, PS[0:65, 0:2048]),
                     wait=[("pe", pvall)], inc="ve")
                oevtok = P.val("ve")
                for t in range(NT):
                    w = [("ve", oevtok)] if t == 0 else ()
                    P.pe(lambda e, t=t: e.transpose(
                        bankb(0)[:, 64 * t:64 * (t + 1)],
                        oT_st[0:64, 128 * t:128 * (t + 1)],
                        ident[0:64, 0:64]), wait=w)
                for t in range(NT):
                    P.pe(lambda e, t=t: e.transpose(
                        bankb(1)[:, 2 * t:2 * t + 1],
                        oT_st[64:65, 128 * t:128 * (t + 1)],
                        ident[64:65, 64:65]))
                P.mark("pe")
                ttok = P.val("pe")
                P.ve(lambda e, h=h: e.tensor_copy(
                    bass.AP(seqout.tensor, seqout.offset + 64 * h,
                            [list(seqout.ap[0]), [1024, NT], [1, 64]]),
                    bankb(0).rearrange("p (t f) -> p t f", f=64)),
                    wait=[("pe", ttok)])
                P.ve(lambda e, h=h: e.tensor_copy(
                    bass.AP(ncols.tensor, ncols.offset + h,
                            [list(ncols.ap[0]), [16, NT]]),
                    bass.AP(PSB.tensor, PSB.offset + 1024,
                            [list(PSB.ap[0]), [2, NT]])), inc="ve")
                otrans_ev = P.val("ve")
            obank_free = otrans_ev

    def emit_p5p6():
        dwv = []
        dwv.append(P.dma("gp", lambda e: e.dma_start(outb_row, b_out.unsqueeze(0))))
        dwv.append(P.dma("gp", lambda e: e.dma_start(mvb_row, b_mv.unsqueeze(0))))
        dwv.append(P.dma("sy", lambda e: e.dma_start(mvout_w[:, 0:8192], mvT_s),
                         wait=scratch_toks))
        dwv.append(P.dma("sy", lambda e: e.dma_start(mvout_w[:, 8192:16384],
                                                     outT_s)))
        P.ve(lambda e: e.memset(statS, 0.0))
        P.ve(lambda e: e.memset(statQ, 0.0))
        for t in range(NT):
            P.ve(lambda e, t=t: e.reciprocal(p5col, ncols[:, 16 * t:16 * (t + 1)]))
            P.ve(lambda e, t=t: e.tensor_tensor(
                seqout[:, 1024 * t:1024 * (t + 1)],
                seqout[:, 1024 * t:1024 * (t + 1)],
                bass.AP(p5col.tensor, p5col.offset,
                        [list(p5col.ap[0]), [1, 16], [0, 64]]), OP.mult),
                inc=("ve" if t == NT - 1 else None))
        nrmtok = P.val("ve")
        ev_guard = {0: P.val("ve"), 1: P.val("ve")}
        for t in range(NT):
            par = t % 2
            b0 = par * 2
            for dc in range(8):
                for half in range(2):
                    w = []
                    if dc == 0 and half == 0:
                        w = dwv + [("ve", max(nrmtok, ev_guard[par]))]
                    P.pe(lambda e, t=t, dc=dc, half=half, b0=b0: e.matmul(
                        bank(b0 + half),
                        xb[:, 2048 * dc + 128 * t:2048 * dc + 128 * (t + 1)],
                        mvout_w[:, 1024 * dc + 512 * half:1024 * dc + 512 * (half + 1)],
                        start=(dc == 0), stop=False), wait=w)
            for half in range(2):
                P.pe(lambda e, half=half, b0=b0: e.matmul(
                    bank(b0 + half), onesrow,
                    mvb_row[:, 512 * half:512 * (half + 1)],
                    start=False, stop=True))
            P.mark("pe")
            pev = P.val("pe")
            P.ve(lambda e, t=t, b0=b0: e.scalar_tensor_tensor(
                seqout[:, 1024 * t:1024 * (t + 1)], bank(b0, 2),
                gcols[:, t:t + 1], seqout[:, 1024 * t:1024 * (t + 1)],
                OP.mult, OP.add), wait=[("pe", pev)], inc="ve")
            ev_guard[par] = P.val("ve")
        cmb_guard = P.val("pe")
        cmb_ev = P.val("ve")
        ev_guard = {0: P.val("ve"), 1: P.val("ve")}
        for t in range(NT):
            par = t % 2
            b0 = par * 2
            for dc in range(8):
                w = [("ve", cmb_ev)] if dc == 0 else ()
                P.pe(lambda e, t=t, dc=dc: e.transpose(
                    bankb(4)[:, 128 * dc:128 * (dc + 1)],
                    seqout[:, 1024 * t + 128 * dc:1024 * t + 128 * (dc + 1)],
                    ident), wait=w)
            P.mark("pe")
            pev = P.val("pe")
            P.ve(lambda e: e.tensor_copy(cmbT, bankb(4)),
                 wait=[("pe", pev), ("pe", cmb_guard)], inc="ve")
            cmbtok = P.val("ve")
            cmb_ev = cmbtok
            for dc in range(8):
                for half in range(2):
                    w = [("ve", cmbtok)] if (dc == 0 and half == 0) else ()
                    P.pe(lambda e, dc=dc, half=half, b0=b0: e.matmul(
                        bank(b0 + half), cmbT[:, 128 * dc:128 * (dc + 1)],
                        mvout_w[:, 8192 + 1024 * dc + 512 * half:
                                8192 + 1024 * dc + 512 * (half + 1)],
                        start=(dc == 0), stop=False), wait=w)
            for half in range(2):
                P.pe(lambda e, half=half, b0=b0: e.matmul(
                    bank(b0 + half), onesrow,
                    outb_row[:, 512 * half:512 * (half + 1)],
                    start=False, stop=True))
            P.mark("pe")
            cmb_guard = P.val("pe")
            pev = P.val("pe")
            P.ve(lambda e, t=t, b0=b0: e.scalar_tensor_tensor(
                xa[:, 1024 * t:1024 * (t + 1)], bank(b0, 2), 1.0,
                xa[:, 1024 * t:1024 * (t + 1)], OP.mult, OP.add,
                accum_out=statS[:, t:t + 1]), wait=[("pe", pev)], inc="ve")
            ev_guard[par] = P.val("ve")
            P.sc(lambda e, t=t: e.activation(
                ln_scrap, xa[:, 1024 * t:1024 * (t + 1)], AF.Square,
                bias=0.0, scale=1.0, accum_out=statQ[:, t:t + 1]),
                wait=[("ve", ev_guard[par])], inc="sc")
        stattok = P.val("sc")
        P.ve(lambda e: e.tensor_scalar(statS, statS, 1.0 / D, 0.0, OP.mult,
                                       OP.add), wait=[("sc", stattok)])
        P.ve(lambda e: e.tensor_scalar(statQ, statQ, 1.0 / D, 0.0, OP.mult,
                                       OP.add))
        P.ve(lambda e: e.tensor_tensor(ln_scrap[:, 0:NT], statS, statS,
                                       OP.mult))
        P.ve(lambda e: e.tensor_tensor(statQ, statQ, ln_scrap[:, 0:NT],
                                       OP.subtract))
        P.ve(lambda e: e.tensor_single_scalar(statQ, statQ, LN_EPS, OP.add),
             inc="ve")
        P.sc(lambda e: e.activation(statR, statQ, AF.Ln, bias=0.0, scale=1.0),
             wait=[("ve", P.val("ve"))], inc="sc")
        P.sc(lambda e: e.activation(statR, statR, AF.Exp, bias=0.0, scale=-0.5),
             inc="sc")
        P.ve(lambda e: e.tensor_tensor(statM, statS, statR, OP.mult),
             wait=[("sc", P.val("sc"))])
        P.ve(lambda e: e.tensor_single_scalar(statM, statM, -1.0, OP.mult),
             inc="ve")
        lntok = P.val("ve")
        for t in range(NT):
            P.sc(lambda e, t=t: e.activation(
                xa[:, 1024 * t:1024 * (t + 1)], xa[:, 1024 * t:1024 * (t + 1)],
                AF.Identity, bias=statM[:, t:t + 1], scale=statR[:, t:t + 1]),
                wait=([("ve", lntok)] if t == 0 else ()),
                inc=("sc" if t == NT - 1 else None))

    def emit_p7(b):
        dwv = []
        dwv.append(P.dma("sy", lambda e: e.dma_start(wguT, wguT_s),
                         wait=scratch_toks))
        dwv.append(P.dma("sy", lambda e: e.dma_start(wdTr, wdT_s)))
        P.ve(lambda e: e.memset(f2S, 0.0))
        P.ve(lambda e: e.memset(f2Q, 0.0))
        h1tok = P.val("sc")
        xqf = PS[:, 3072:4096]
        xqT_b = [xqT, xqT1]
        dsl_b = [dsl0, xq_t]
        hqb_b = [hqb0, hqb1]
        fc_b = [fcol, fcol2]
        xqT_toks = {}
        xqT_free = {0: P.val("pe"), 1: P.val("pe")}
        st = {"b6": P.val("ve"), "b7": P.val("ve"),
              "xqt_rd": P.val("pe"),
              "dsl": {0: P.val("pe"), 1: P.val("pe")},
              "hqb": {0: P.val("pe"), 1: P.val("pe")}}

        def emit_xq(t):
            fc = fc_b[t % 2]
            xat = xa[:, 1024 * t:1024 * (t + 1)]
            P.ve(lambda e, xat=xat, fc=fc: e.reduce_max(
                fc[:, 0:1], xat, axis=AX, apply_absolute_value=True),
                wait=[("sc", h1tok)] if t == 0 else ())
            P.ve(lambda e, fc=fc: e.tensor_single_scalar(
                fc[:, 1:2], fc[:, 0:1], 1e-5, OP.max))
            P.ve(lambda e, fc=fc: e.reciprocal(fc[:, 2:3], fc[:, 1:2]))
            P.ve(lambda e, fc=fc: e.tensor_single_scalar(
                fc[:, 2:3], fc[:, 2:3], 127.0, OP.mult))
            P.ve(lambda e, fc=fc: e.tensor_single_scalar(
                fc[:, 3:4], fc[:, 1:2], 1.0 / 127.0, OP.mult))
            P.ve(lambda e, fc=fc: e.tensor_tensor(
                fc[:, 4:5], fc[:, 3:4], mwcol[:, 0:1], OP.mult))
            P.ve(lambda e, fc=fc: e.tensor_tensor(
                fc[:, 5:6], fc[:, 3:4], mwcol[:, 1:2], OP.mult), inc="ve")
            colstok = P.val("ve")
            P.sc(lambda e, xat=xat, fc=fc: e.activation(
                xqf, xat, AF.Identity, bias=magiccol, scale=fc[:, 2:3]),
                wait=[("ve", colstok), ("ve", st["b6"]), ("ve", st["b7"])],
                inc="sc")
            P.ve(lambda e: e.tensor_single_scalar(
                xq_t, xqf, MAGIC, OP.subtract),
                wait=[("sc", P.val("sc")), ("pe", st["dsl"][1])], inc="ve")
            xqtok = P.val("ve")
            for dc in range(8):
                w = [("ve", xqtok)] if dc == 0 else ()
                P.pe(lambda e, dc=dc: e.transpose(
                    bankb(6)[:, 128 * dc:128 * (dc + 1)],
                    xq_t[:, 128 * dc:128 * (dc + 1)], ident), wait=w)
            P.mark("pe")
            st["xqt_rd"] = P.val("pe")
            P.ve(lambda e, t=t: e.tensor_copy(xqT_b[t % 2], bankb(6)),
                 wait=[("pe", st["xqt_rd"]), ("pe", xqT_free[t % 2])],
                 inc="ve")
            st["b6"] = P.val("ve")
            xqT_toks[t] = P.val("ve")

        emit_xq(0)
        prev_res = P.val("ve")
        hread_g = {0: P.val("ve"), 1: P.val("ve")}
        silu_g = {0: P.val("sc"), 1: P.val("sc")}
        stt_g = [P.val("ve")]
        gu_pre = {}

        def emit_gu(t, og, extra=()):
            gb = (og % 2) * 2
            xT = xqT_b[t % 2]
            for dc in range(8):
                w = []
                if dc == 0:
                    w = [("ve", xqT_toks[t]), ("sc", silu_g[og % 2]),
                         ("ve", hread_g[og % 2])] + list(extra)
                P.pe(lambda e, og=og, dc=dc, gb=gb, xT=xT: e.matmul(
                    bank(gb), xT[:, 128 * dc:128 * (dc + 1)],
                    wguT[:, 8192 * dc + 512 * og:8192 * dc + 512 * (og + 1)],
                    start=(dc == 0), stop=(dc == 7)), wait=w)
            for dc in range(8):
                P.pe(lambda e, og=og, dc=dc, gb=gb, xT=xT: e.matmul(
                    bank(gb + 1), xT[:, 128 * dc:128 * (dc + 1)],
                    wguT[:, 8192 * dc + 4096 + 512 * og:
                         8192 * dc + 4096 + 512 * (og + 1)],
                    start=(dc == 0), stop=(dc == 7)))
            P.mark("pe")
            return P.val("pe")

        for t in range(NT):
            xat = xa[:, 1024 * t:1024 * (t + 1)]
            fc = fc_b[t % 2]
            for og in range(8):
                gb = (og % 2) * 2
                if (t, og) in gu_pre:
                    gutok = gu_pre.pop((t, og))
                else:
                    gutok = emit_gu(t, og,
                                    extra=(dwv if t == 0 and og == 0 else ()))
                if og == 7:
                    xqT_free[t % 2] = gutok
                P.sc(lambda e, gb=gb, fc=fc: e.activation(
                    silu_t, bank(gb), AF.Silu, bias=0.0, scale=fc[:, 4:5]),
                    wait=[("pe", gutok), ("ve", stt_g[0])], inc="sc")
                silu_g[og % 2] = P.val("sc")
                P.ve(lambda e, og=og, gb=gb, fc=fc: e.scalar_tensor_tensor(
                    htile[:, 512 * og:512 * (og + 1)], bank(gb + 1),
                    fc[:, 5:6], silu_t, OP.mult, OP.mult),
                    wait=[("sc", P.val("sc"))], inc="ve")
                stt_g[0] = P.val("ve")
                P.ve(lambda e, og=og: e.reduce_max(
                    hpart[:, og:og + 1], htile[:, 512 * og:512 * (og + 1)],
                    axis=AX, apply_absolute_value=True), inc="ve")
                hread_g[og % 2] = P.val("ve")
                if og == 2 and t + 1 < NT:
                    emit_xq(t + 1)
            if t + 1 < NT:
                gu_pre[(t + 1, 0)] = emit_gu(t + 1, 0)
                gu_pre[(t + 1, 1)] = emit_gu(t + 1, 1)
            P.ve(lambda e, fc=fc: e.reduce_max(fc[:, 6:7], hpart, axis=AX),
                 wait=[("ve", hread_g[1])])
            P.ve(lambda e, fc=fc: e.tensor_single_scalar(
                fc[:, 6:7], fc[:, 6:7], 1e-5, OP.max))
            P.ve(lambda e, fc=fc: e.reciprocal(fc[:, 7:8], fc[:, 6:7]))
            P.ve(lambda e, fc=fc: e.tensor_single_scalar(
                fc[:, 7:8], fc[:, 7:8], 127.0, OP.mult))
            P.ve(lambda e, fc=fc: e.tensor_single_scalar(
                fc[:, 0:1], fc[:, 6:7], 1.0 / 127.0, OP.mult))
            P.ve(lambda e, fc=fc: e.tensor_tensor(
                fc[:, 0:1], fc[:, 0:1], mwcol[:, 2:3], OP.mult), inc="ve")
            hsctok = P.val("ve")
            for icq in range(4):
                sl = icq % 2
                P.sc(lambda e, icq=icq, fc=fc: e.activation(
                    htile[:, 1024 * icq:1024 * (icq + 1)],
                    htile[:, 1024 * icq:1024 * (icq + 1)],
                    AF.Identity, bias=magiccol, scale=fc[:, 7:8]),
                    wait=[("ve", hsctok)] if icq == 0 else (), inc="sc")
                P.ve(lambda e, icq=icq, sl=sl: e.tensor_single_scalar(
                    hqb_b[sl],
                    htile[:, 1024 * icq:1024 * (icq + 1)],
                    MAGIC, OP.subtract),
                    wait=[("sc", P.val("sc")), ("pe", st["hqb"][sl])],
                    inc="ve")
                hqtok = P.val("ve")
                for ii in range(8):
                    w = []
                    if ii == 0:
                        w = [("ve", hqtok), ("ve", st["b7"])]
                    P.pe(lambda e, sl=sl, ii=ii: e.transpose(
                        bankb(7)[:, 128 * ii:128 * (ii + 1)],
                        hqb_b[sl][:, 128 * ii:128 * (ii + 1)],
                        ident), wait=w)
                P.mark("pe")
                st["hqb"][sl] = P.val("pe")
                wv = [("pe", st["hqb"][sl]), ("pe", st["dsl"][sl])]
                if sl == 1:
                    wv.append(("pe", st["xqt_rd"]))
                P.ve(lambda e, sl=sl: e.tensor_copy(dsl_b[sl], bankb(7)),
                     wait=wv, inc="ve")
                st["b7"] = P.val("ve")
                slot_ev = P.val("ve")
                for ii in range(8):
                    ic = icq * 8 + ii
                    for half in range(2):
                        w = []
                        if ii == 0 and half == 0:
                            w = [("ve", slot_ev)]
                        P.pe(lambda e, ic=ic, ii=ii, half=half, sl=sl:
                             e.matmul(
                                 bank(4 + half),
                                 dsl_b[sl][:, 128 * ii:128 * (ii + 1)],
                                 wdTr[:, 1024 * ic + 512 * half:
                                      1024 * ic + 512 * (half + 1)],
                                 start=(ic == 0), stop=(ic == 31)), wait=w)
                P.mark("pe")
                st["dsl"][sl] = P.val("pe")
            downtok = P.val("pe")
            P.ve(lambda e, t=t, xat=xat, fc=fc: e.scalar_tensor_tensor(
                xat, bank(4, 2), fc[:, 0:1], xat, OP.mult, OP.add,
                accum_out=f2S[:, t:t + 1]),
                wait=[("pe", downtok), ("ve", prev_res)], inc="ve")
            prev_res = P.val("ve")
            P.sc(lambda e, t=t, xat=xat: e.activation(
                sq_scr, xat, AF.Square, bias=0.0, scale=1.0,
                accum_out=f2Q[:, t:t + 1]),
                wait=[("ve", prev_res)], inc="sc")
        stattok = P.val("sc")
        P.ve(lambda e: e.tensor_scalar(f2S, f2S, 1.0 / D, 0.0, OP.mult, OP.add),
             wait=[("sc", stattok)])
        P.ve(lambda e: e.tensor_scalar(f2Q, f2Q, 1.0 / D, 0.0, OP.mult, OP.add))
        P.ve(lambda e: e.tensor_tensor(f2M, f2S, f2S, OP.mult))
        P.ve(lambda e: e.tensor_tensor(f2Q, f2Q, f2M, OP.subtract))
        P.ve(lambda e: e.tensor_single_scalar(f2Q, f2Q, LN_EPS, OP.add),
             inc="ve")
        P.sc(lambda e: e.activation(f2R, f2Q, AF.Ln, bias=0.0, scale=1.0),
             wait=[("ve", P.val("ve"))], inc="sc")
        P.sc(lambda e: e.activation(f2R, f2R, AF.Exp, bias=0.0, scale=-0.5),
             inc="sc")
        P.ve(lambda e: e.tensor_tensor(f2M, f2S, f2R, OP.mult),
             wait=[("sc", P.val("sc"))])
        P.ve(lambda e: e.tensor_single_scalar(f2M, f2M, -1.0, OP.mult),
             inc="ve")
        ln2tok = P.val("ve")
        for t in range(NT):
            P.sc(lambda e, t=t: e.activation(
                xa[:, 1024 * t:1024 * (t + 1)], xa[:, 1024 * t:1024 * (t + 1)],
                AF.Identity, bias=f2M[:, t:t + 1], scale=f2R[:, t:t + 1]),
                wait=([("ve", ln2tok)] if t == 0 else ()), inc="sc")
            sctok = P.val("sc")
            out_toks[b].append(P.dma("sy", lambda e, t=t, b=b: e.dma_start(
                y_d[b].rearrange("(t q) d -> q t d", q=128)[:, t, :],
                xa[:, 1024 * t:1024 * (t + 1)]),
                wait=[("sc", sctok)]))

    def stop(tag):
        dbg_dump(tag)
        if stop_after == tag:
            P.sy(lambda e: None, wait=P.dma_all())
            P.barrier()
            P.build_block()
            return True
        return False

    emit_prep_ffn_dma("g")
    emit_consts()
    if stop("consts"):
        return P
    emit_prep_small()
    emit_prep_big()
    emit_prep_ffn()
    P.sy(lambda e: None, wait=P.dma_all())
    P.barrier()
    if stop("prep"):
        return P

    for b in range(B_LOC):
        emit_p0(b)
        P.barrier()
        if stop(f"p0_{b}"):
            return P
        emit_p2()
        P.barrier()
        if stop(f"p2_{b}"):
            return P
        emit_p3()
        P.barrier()
        if stop(f"p3_{b}"):
            return P
        emit_p4()
        P.barrier()
        if stop(f"p4_{b}"):
            return P
        emit_p5p6()
        P.barrier()
        if stop(f"p56_{b}"):
            return P
        emit_p7(b)
        P.barrier()
        if stop(f"p7_{b}"):
            return P

    P.sy(lambda e: None, wait=P.dma_all())
    P.barrier()
    P.build_block()
    return P


_NC_CACHE = {}


def _build():
    if "nc" not in _NC_CACHE:
        nc = bass.Bass("TRN2", target_bir_lowering=False)
        build_core(nc)
        _NC_CACHE["nc"] = nc
    return _NC_CACHE["nc"]


def kernel(**inputs):
    x = np.ascontiguousarray(inputs["L0_attn_x"], dtype=np.float32)
    common = {
        "w_qkv": np.ascontiguousarray(inputs["L0_attn_qkv_w"], np.float32),
        "b_qkv": np.ascontiguousarray(inputs["L0_attn_qkv_b"], np.float32),
        "w_w1": np.ascontiguousarray(inputs["L0_attn_w1w"], np.float32),
        "w_w2": np.ascontiguousarray(inputs["L0_attn_w2w"], np.float32),
        "w_r1": np.ascontiguousarray(inputs["L0_attn_r1w"], np.float32),
        "w_r2": np.ascontiguousarray(inputs["L0_attn_r2w"], np.float32),
        "w_mv": np.ascontiguousarray(inputs["L0_attn_memv_w"], np.float32),
        "b_mv": np.ascontiguousarray(inputs["L0_attn_memv_b"], np.float32),
        "w_mg": np.ascontiguousarray(inputs["L0_attn_memg_w"], np.float32),
        "b_mg": np.ascontiguousarray(inputs["L0_attn_memg_b"], np.float32),
        "mem_scale": np.ascontiguousarray(inputs["L0_attn_mem_scale"], np.float32),
        "iter_mix": np.ascontiguousarray(
            np.asarray(inputs["L0_attn_iter_mix"], np.float32).reshape(1)),
        "w_out": np.ascontiguousarray(inputs["L0_attn_out_w"], np.float32),
        "b_out": np.ascontiguousarray(inputs["L0_attn_out_b"], np.float32),
        "dlog": np.ascontiguousarray(inputs["L0_attn_decay_logits"], np.float32),
        "w_g": np.ascontiguousarray(inputs["L0_ffn_w_gate"], np.float32),
        "w_u": np.ascontiguousarray(inputs["L0_ffn_w_up"], np.float32),
        "w_dn": np.ascontiguousarray(inputs["L0_ffn_w_down"], np.float32),
    }
    nc = _build()
    in_maps = []
    for c in range(NCORES):
        m = dict(common)
        m["x"] = np.ascontiguousarray(x[c * B_LOC:(c + 1) * B_LOC])
        in_maps.append(m)
    res = run_bass_kernel_spmd(nc, in_maps, core_ids=list(range(NCORES)))
    y = np.concatenate([r["y"] for r in res.results], axis=0)
    return y.reshape(B_GLOB, T, D)
